# version 18
# baseline (speedup 1.0000x reference)
import numpy as np
import ml_dtypes
import concourse.bass as bass
import concourse.mybir as mybir
from concourse.bass_utils import run_bass_kernel_spmd

F32 = mybir.dt.float32
BF16 = mybir.dt.bfloat16
AF = mybir.ActivationFunctionType
ALU = mybir.AluOpType
AX = mybir.AxisListType

P = 128
TT = 16
IC = 16
H = 16
DIM = 64
E = 8
NTOK = 2048
D = 2048
NTG = 2
NOH = 4
EPS_LN = 1e-5
EPS_AT = 1e-6
STOP = "full"


class Sem:
    def __init__(self, nc, name):
        self.h = nc.alloc_semaphore(name)
        self.n = 0

    def inc(self, k=1):
        self.n += k
        return self.n


class Prog:
    ENGS = ("T", "V", "S", "P", "Y")

    def __init__(self, nc):
        self.nc = nc
        self.streams = {e: [] for e in self.ENGS}
        self.tm = Sem(nc, "tm")
        self.vm = Sem(nc, "vm")
        self.sm = Sem(nc, "sm")
        self.cm = Sem(nc, "cm")
        self.cy = Sem(nc, "cy")
        self.cp = Sem(nc, "cp")
        self.xl = [Sem(nc, "xl0"), Sem(nc, "xl1")]
        self.ws = [Sem(nc, "ws0"), Sem(nc, "ws1")]
        self.pm = Sem(nc, "pm")
        self.yk = [Sem(nc, "yk0"), Sem(nc, "yk1")]
        self.wv = [Sem(nc, "wv0"), Sem(nc, "wv1")]
        self.kva = Sem(nc, "kva")
        self.kvb = Sem(nc, "kvb")
        self.wf = [Sem(nc, "wf0"), Sem(nc, "wf1")]
        self.ai = [Sem(nc, "ai0"), Sem(nc, "ai1")]
        self.asn = [Sem(nc, "as0"), Sem(nc, "as1")]

    def emit(self, eng, fn, waits=(), incs=()):
        ws = tuple((s, v) for (s, v) in waits if v)
        self.streams[eng].append((ws, fn, tuple(incs)))

    def op(self, eng, fn, waits=()):
        sem = {"V": self.vm, "S": self.sm}[eng]
        v = sem.inc(1)
        self.emit(eng, fn, waits, ((sem, 1),))
        return v

    def mm(self, fn, waits=(), inc=False):
        if inc:
            v = self.tm.inc(1)
            self.emit("T", fn, waits, ((self.tm, 1),))
            return v
        self.emit("T", fn, waits, ())
        return None

    def dma(self, q, sem, fn, waits=()):
        v = sem.inc(16)
        self.emit(q, fn, waits, ((sem, 16),))
        return v

    def quiesce(self):
        tails = [(self.tm, self.tm.n), (self.vm, self.vm.n), (self.sm, self.sm.n),
                 (self.cy, self.cy.n), (self.cp, self.cp.n),
                 (self.ws[0], self.ws[0].n), (self.ws[1], self.ws[1].n),
                 (self.pm, self.pm.n),
                 (self.yk[0], self.yk[0].n), (self.yk[1], self.yk[1].n),
                 (self.xl[0], self.xl[0].n), (self.xl[1], self.xl[1].n),
                 (self.wv[0], self.wv[0].n), (self.wv[1], self.wv[1].n),
                 (self.kva, self.kva.n), (self.kvb, self.kvb.n),
                 (self.cm, self.cm.n),
                 (self.wf[0], self.wf[0].n), (self.wf[1], self.wf[1].n),
                 (self.ai[0], self.ai[0].n), (self.ai[1], self.ai[1].n),
                 (self.asn[0], self.asn[0].n), (self.asn[1], self.asn[1].n)]
        for s, v in tails:
            if v:
                self.emit("Y", lambda e, s=s, v=v: e.wait_ge(s.h, v))

    def realize(self):
        def body(eng_key):
            stream = self.streams[eng_key]

            def _(e):
                for waits, fn, incs in stream:
                    for s, v in waits:
                        e.wait_ge(s.h, v)
                    inst = fn(e)
                    for s, k in incs:
                        inst.then_inc(s.h, k)
            return _
        return body


def build(nc):
    x_d = nc.dram_tensor("x", [NTOK, D], F32, kind="ExternalInput")
    wq_d = nc.dram_tensor("wq", [D, D], F32, kind="ExternalInput")
    wk_d = nc.dram_tensor("wk", [D, D], F32, kind="ExternalInput")
    wvT_d = nc.dram_tensor("wvT", [D, D], F32, kind="ExternalInput")
    woT_d = nc.dram_tensor("woT", [D, D], F32, kind="ExternalInput")
    wT_d = nc.dram_tensor("wT", [E, D, D], F32, kind="ExternalInput")
    wmapqT_d = nc.dram_tensor("wmapqT", [P, DIM], F32, kind="ExternalInput")
    wmapkT_d = nc.dram_tensor("wmapkT", [P, DIM], F32, kind="ExternalInput")
    wg_d = nc.dram_tensor("wg", [D, E], F32, kind="ExternalInput")
    bq_d = nc.dram_tensor("bq", [1, D], F32, kind="ExternalInput")
    bk_d = nc.dram_tensor("bk", [1, D], F32, kind="ExternalInput")
    bv_d = nc.dram_tensor("bv", [1, D], F32, kind="ExternalInput")
    bo_d = nc.dram_tensor("bo", [1, D], F32, kind="ExternalInput")
    be_d = nc.dram_tensor("be", [E, D], F32, kind="ExternalInput")
    out_d = nc.dram_tensor("out", [NTOK, D], F32, kind="ExternalOutput")

    cc_in = nc.dram_tensor("cc_in", [P, 1032], F32, kind="Internal")
    cc_out = nc.dram_tensor("cc_out", [P, 1032], F32, kind="Internal")

    ident_d = nc.inline_tensor(np.eye(P, dtype=ml_dtypes.bfloat16), name="identbf")
    ones_d = nc.inline_tensor(np.ones((1, 512), dtype=ml_dtypes.bfloat16), name="onesbf")
    onec_d = nc.inline_tensor(np.ones((P, 1), dtype=ml_dtypes.bfloat16), name="onecbf")

    A = nc.alloc_sbuf_tensor
    AT = nc.alloc_sbuf_tensor_at
    addr = lambda h: nc.lookup_mloc(h).addr

    xn = A("xn", [P, IC * NTOK], BF16)
    q2 = A("q2", [P, 8 * NTOK], BF16)
    rc = A("rc", [P, IC * 1024], BF16)
    w2q = AT("w2q", [P, IC * 1024], BF16, offset=addr(rc))
    k2 = AT("k2", [P, TT * 1024], BF16, offset=addr(rc))
    attn_f = AT("attn_f", [P, H * 1024], BF16, offset=addr(rc))
    rd = A("rd", [P, IC * 1024], BF16)
    w2k = AT("w2k", [P, IC * 1024], BF16, offset=addr(rd))
    wvg = [AT("wvg0", [P, IC * 512], BF16, offset=addr(rd)),
           AT("wvg1", [P, IC * 512], BF16, offset=addr(rd) + IC * 512 * 2)]
    acc0 = AT("acc0", [P, 8 * 512], F32, offset=addr(rd))
    acc1 = AT("acc1", [P, 8 * 512], F32, offset=addr(rd) + 8 * 512 * 4)
    accs = [acc0, acc1]
    re = A("re", [P, 8192], BF16)
    xtile = AT("xtile", [P, D], F32, offset=addr(re))
    wffn = [AT("wffn0", [P, 4096], BF16, offset=addr(re)),
            AT("wffn1", [P, 4096], BF16, offset=addr(re) + 4096 * 2)]
    wst = [AT("wst0", [P, D], BF16, offset=addr(re) + 8192),
           AT("wst1", [P, D], BF16, offset=addr(re) + 12288)]
    ascr = A("ascr", [P, 4096], BF16)
    asb = [AT("asb0", [P, D], BF16, offset=addr(ascr)),
           AT("asb1", [P, D], BF16, offset=addr(ascr) + 4096)]
    elu_r = [AT("elu_r0", [P, 512], F32, offset=addr(ascr)),
             AT("elu_r1", [P, 512], F32, offset=addr(ascr) + 4096)]
    elu_m = [AT("elu_m0", [P, 512], F32, offset=addr(ascr) + 2048),
             AT("elu_m1", [P, 512], F32, offset=addr(ascr) + 4096 + 2048)]
    xtile1 = AT("xtile1", [P, D], F32, offset=addr(ascr))
    xts = [None, None]

    ident = A("ident", [P, P], BF16)
    ones = A("ones", [1, 512], BF16)
    onec = A("onec", [P, 1], BF16)
    wg_bf = A("wg_bf", [P, IC * E], BF16)
    wmq = A("wmq", [P, DIM], BF16)
    wmk = A("wmk", [P, DIM], BF16)
    bq_row = AT("bq_row", [1, D], BF16, offset=addr(rc))
    bk_row = AT("bk_row", [1, D], BF16, offset=addr(rc) + 4096)
    bv_row = A("bv_row", [1, D], BF16)
    b9 = A("b9", [9, D], BF16)
    bq_col = A("bq_col", [P, H], BF16)
    bk_col = A("bk_col", [P, H], BF16)
    b2q_row = A("b2q_row", [1, 1024], BF16)
    b2k_row = A("b2k_row", [1, 1024], BF16)
    g_all = A("g_all", [P, TT * E], F32)
    lg_all = AT("lg_all", [P, TT * E], F32, offset=addr(re))
    mtB = AT("mtB", [P, TT * E], F32, offset=addr(re) + 512)
    caB = AT("caB", [P, TT * E], F32, offset=addr(re) + 1024)
    cbB = AT("cbB", [P, TT * E], F32, offset=addr(re) + 1536)
    ccB = AT("ccB", [P, TT * E], F32, offset=addr(re) + 2048)
    etB = AT("etB", [P, TT * E], F32, offset=addr(re) + 2560)
    m16 = A("m16", [P, TT], F32)
    ssum16 = A("ssum16", [P, TT], F32)
    gate16 = A("gate16", [P, TT], F32)
    g9bf = A("g9bf", [P, 9], BF16)
    gT9 = A("gT9", [9, TT * P], BF16)
    kv_f32 = AT("kv_f32", [P, 1032], F32, offset=addr(re) + 8192)
    kv_z = A("kv_z", [P, 2048], BF16)
    ksum_z = A("ksum_z", [P, 2 * E], BF16)
    nmean = A("nmean", [P, 1], F32)
    nsum16 = A("nsum16", [P, TT], F32)
    sumsq16 = A("sumsq16", [P, TT], F32)
    nmean16 = A("nmean16", [P, TT], F32)
    rstd16 = A("rstd16", [P, TT], F32)
    vtmp16 = A("vtmp16", [P, TT], F32)
    sumsq = A("sumsq", [P, 1], F32)
    nsum = A("nsum", [P, 1], F32)
    varv = A("varv", [P, 1], F32)
    sq_t = A("sq_t", [P, 1], F32)
    rstd = A("rstd", [P, 1], F32)
    drstd = A("drstd", [P, P], BF16)
    xc_bf = AT("xc_bf", [P, D], BF16, offset=addr(re) + 8192)
    lg = A("lg", [P, E], F32)
    mxv = A("mxv", [P, 1], F32)
    nmx = A("nmx", [P, 1], F32)
    mt_t = A("mt_t", [P, E], F32)
    ca_t = A("ca_t", [P, E], F32)
    cb_t = A("cb_t", [P, E], F32)
    cc_t = A("cc_t", [P, E], F32)
    et_t = A("et_t", [P, E], F32)
    ssum = A("ssum", [P, 1], F32)
    gate = A("gate", [P, 1], F32)
    rnorm2 = [A("rnorm0", [P, H], F32), A("rnorm1", [P, H], F32)]
    sq_trash2 = [AT("sq_trash0", [P, D], BF16, offset=addr(q2) + 4096),
                 AT("sq_trash1", [P, D], BF16, offset=addr(q2) + 8192)]
    wkst = [AT("wkst0", [P, D], F32, offset=addr(q2) + 12288),
            AT("wkst1", [P, D], F32, offset=addr(q2) + 20480)]

    ps = [nc.alloc_psum_tensor(f"ps{k}", [P, 512], F32) for k in range(8)]

    pr = Prog(nc)
    xn3 = xn.ap().rearrange("p (i t) -> p i t", t=NTOK)

    pr.dma("Y", pr.cy, lambda e: e.dma_start(out=ident[:, :], in_=ident_d[:, :]))
    pr.dma("Y", pr.cy, lambda e: e.dma_start(out=ones[:, :], in_=ones_d[:, :]))
    cy_all = pr.dma("Y", pr.cy, lambda e: e.dma_start(out=onec[:, :], in_=onec_d[:, :]))
    wg_ap = wg_d.ap().rearrange("(c p) e -> p c e", p=P)
    wg_bf3 = wg_bf.ap().rearrange("p (c e) -> p c e", e=E)
    pr.dma("P", pr.cp, lambda e: e.dma_start(out=wg_bf3, in_=wg_ap))
    pr.dma("P", pr.cp, lambda e: e.dma_start(out=wmq[:, :], in_=wmapqT_d[:, :]))
    pr.dma("P", pr.cp, lambda e: e.dma_start(out=wmk[:, :], in_=wmapkT_d[:, :]))
    pr.dma("P", pr.cp, lambda e: e.dma_start(out=bq_row[:, :], in_=bq_d[:, :]))
    pr.dma("P", pr.cp, lambda e: e.dma_start(out=bk_row[:, :], in_=bk_d[:, :]))
    pr.dma("P", pr.cp, lambda e: e.dma_start(out=bv_row[:, :], in_=bv_d[:, :]))
    pr.dma("P", pr.cp, lambda e: e.dma_start(out=b9[0:E, :], in_=be_d[:, :]))
    consts_done = pr.dma("P", pr.cp, lambda e: e.dma_start(out=b9[E:E + 1, :], in_=bo_d[:, :]))

    def bias_cols(row, col_sb, wait0):
        tv = None
        for h in range(H):
            tv = pr.mm(lambda e, h=h, row=row: e.matmul(
                ps[0][:, h:h + 1], row[0:1, h * P:(h + 1) * P], ones[0:1, 0:1],
                start=(h == 0), stop=(h == H - 1)),
                waits=wait0 if h == 0 else (), inc=(h == H - 1))
        return pr.op("V", lambda e, col_sb=col_sb: e.tensor_copy(
            out=col_sb[:, :], in_=ps[0][:, 0:H]), waits=((pr.tm, tv),))

    v_bqc = bias_cols(bq_row, bq_col,
                      ((pr.cp, consts_done), (pr.cy, cy_all)))
    v_bkc = bias_cols(bk_row, bk_col, ((pr.vm, v_bqc),))

    def b2_build(col_sb, wmap_sb, row_out, wait_v):
        outv = None
        for half in range(2):
            tv = None
            for hh in range(8):
                h = half * 8 + hh
                wl = []
                if half == 0 and hh == 0:
                    wl.append((pr.vm, wait_v))
                if half == 1 and hh == 0:
                    wl.append((pr.vm, outv))
                tv = pr.mm(lambda e, h=h, hh=hh, col_sb=col_sb, wmap_sb=wmap_sb: e.matmul(
                    ps[1][0:1, hh * DIM:(hh + 1) * DIM],
                    col_sb[:, h:h + 1], wmap_sb[:, :],
                    start=(hh == 0), stop=(hh == 7)),
                    waits=tuple(wl), inc=(hh == 7))
            outv = pr.op("V", lambda e, half=half, row_out=row_out: e.tensor_copy(
                out=row_out[0:1, half * 512:(half + 1) * 512], in_=ps[1][0:1, 0:512]),
                waits=((pr.tm, tv),))
        return outv

    v_b2q = b2_build(bq_col, wmq, b2q_row, v_bqc)
    v_b2k = b2_build(bk_col, wmk, b2k_row, v_b2q)

    wst_free = [0, 0]
    stage_idx = [0]
    w2q3 = w2q.ap().rearrange("p (i m) -> p i m", m=1024)
    w2k3 = w2k.ap().rearrange("p (i m) -> p i m", m=1024)
    w2_evs = {"q": [], "k": []}

    pool_cast_last = [0, 0]
    w2_all_evs = []

    def emit_w2_chunk(which, h, ev_extra=0):
        src_d, wmap_sb, dst3 = ((wq_d, wmq, w2q3) if which == "q"
                                else (wk_d, wmk, w2k3))
        evs = w2_evs[which]
        b = stage_idx[0] % 2
        stage_idx[0] += 1
        if which == "q":
            ld = pr.dma("P", pr.ws[b], lambda e, h=h, b=b, src_d=src_d: e.dma_start(
                out=wst[b][:, :], in_=src_d[h * P:(h + 1) * P, :]),
                waits=(((pr.vm, wst_free[b]),) if wst_free[b] else ()))
            mm_wait = (pr.ws[b], ld)
        else:
            kb = h % 2
            wl_ld = []
            if pool_cast_last[kb]:
                wl_ld.append(pool_cast_last[kb])
            ldf = pr.dma("Y", pr.yk[kb], lambda e, h=h, kb=kb, src_d=src_d: e.dma_start(
                out=wkst[kb][:, :], in_=src_d[h * P:(h + 1) * P, :]),
                waits=tuple(wl_ld))
            cwl = [(pr.yk[kb], ldf)]
            if wst_free[b]:
                cwl.append((pr.vm, wst_free[b]))
            if kb == 0:
                cv = pr.op("V", lambda e, b=b, kb=kb: e.tensor_copy(
                    out=wst[b][:, :], in_=wkst[kb][:, :]), waits=tuple(cwl))
                mm_wait = (pr.vm, cv)
                pool_cast_last[kb] = (pr.vm, cv)
            else:
                cv = pr.op("S", lambda e, b=b, kb=kb: e.copy(
                    out=wst[b][:, :], in_=wkst[kb][:, :]), waits=tuple(cwl))
                mm_wait = (pr.sm, cv)
                pool_cast_last[kb] = (pr.sm, cv)
        allv = w2_all_evs
        for i in range(IC):
            bank = 4 + (h % 2) * 2 + i // 8
            wl = []
            if i == 0:
                wl.append(mm_wait)
                wl.append((pr.cp, consts_done))
                if allv:
                    wl.append((pr.vm, allv[-1]))
                else:
                    wl.append((pr.vm, v_b2k))
            pr.mm(lambda e, i=i, b=b, bank=bank, wmap_sb=wmap_sb: e.matmul(
                ps[bank][:, (i % 8) * DIM:(i % 8 + 1) * DIM],
                wst[b][:, i * P:(i + 1) * P], wmap_sb[:, :],
                start=(i % 8 == 0), stop=(i % 8 == 7)),
                waits=tuple(wl), inc=(i == 7 or i == IC - 1))
        tmv = pr.tm.n
        evl = None
        for half in range(2):
            bank = 4 + (h % 2) * 2 + half
            wle = [(pr.tm, tmv - (1 - half))]
            if ev_extra and half == 0:
                wle.append((pr.sm, ev_extra))
            evl = pr.op("V", lambda e, h=h, half=half, bank=bank, dst3=dst3: e.tensor_copy(
                out=dst3[:, half * 8:(half + 1) * 8, h * DIM:(h + 1) * DIM],
                in_=ps[bank].ap().rearrange("p (i m) -> p i m", m=DIM)),
                waits=tuple(wle))
        evs.append(evl)
        w2_all_evs.append(evl)
        wst_free[b] = evl

    xts[0], xts[1] = xtile, xtile1
    xl_v = [0, 0]
    xl_s = [0, 0]
    for t in range(TT):
        xt = xts[t % 2]
        ld = pr.dma("Y", pr.xl[t % 2], lambda e, t=t, xt=xt: e.dma_start(
            out=xt[:, :], in_=x_d[t * P:(t + 1) * P, :]),
            waits=((pr.vm, xl_v[t % 2]), (pr.sm, xl_s[t % 2])))
        v1 = pr.op("V", lambda e, t=t, xt=xt: e.reduce_sum(
            out=nsum16[:, t:t + 1], in_=xt[:, :], axis=AX.X, negate=True),
            waits=((pr.xl[t % 2], ld),))
        s1 = pr.op("S", lambda e, t=t, xt=xt: e.activation(
            out=sq_trash2[t % 2][:, :], in_=xt[:, :], func=AF.Square,
            accum_out=sumsq16[:, t:t + 1]),
            waits=((pr.xl[t % 2], ld), (pr.sm, xl_s[t % 2])))
        xl_v[t % 2], xl_s[t % 2] = v1, s1
        emit_w2_chunk("q", t)
        emit_w2_chunk("k", t)
    vw2q = w2_evs["q"][-1]
    vw2k = w2_evs["k"][-1]
    w2ev_last = w2_all_evs[-1]
    w2_last_tm = pr.tm.n
    st_v = pr.vm.n
    st_s = pr.sm.n
    u1 = pr.op("V", lambda e: e.tensor_scalar_mul(out=nmean16[:, :], in0=nsum16[:, :],
                                                  scalar1=1.0 / D),
               waits=((pr.vm, st_v),))
    u2 = pr.op("V", lambda e: e.tensor_scalar(out=vtmp16[:, :], in0=nmean16[:, :],
                                              scalar1=nmean16[:, :] if False else None,
                                              scalar2=None, op0=ALU.mult)
               if False else e.tensor_mul(out=vtmp16[:, :], in0=nmean16[:, :],
                                          in1=nmean16[:, :]),
               waits=((pr.vm, u1),))
    u3 = pr.op("V", lambda e: e.tensor_scalar(out=vtmp16[:, :], in0=vtmp16[:, :],
                                              scalar1=EPS_LN, scalar2=None,
                                              op0=ALU.subtract),
               waits=((pr.vm, u2),))
    u4 = pr.op("V", lambda e: e.scalar_tensor_tensor(
        out=vtmp16[:, :], in0=sumsq16[:, :], scalar=1.0 / D, in1=vtmp16[:, :],
        op0=ALU.mult, op1=ALU.subtract),
        waits=((pr.vm, u3), (pr.sm, st_s)))
    u5 = pr.op("S", lambda e: e.activation(out=vtmp16[:, :], in_=vtmp16[:, :],
                                           func=AF.Ln),
               waits=((pr.vm, u4),))
    u6 = pr.op("S", lambda e: e.activation(out=rstd16[:, :], in_=vtmp16[:, :],
                                           func=AF.Exp, bias=0.0, scale=-0.5),
               waits=((pr.sm, u5),))

    elu_free = [0, 0]
    elu_s_last = [0]

    def elu_tile(bank, dst_ap_fn, buf, tm_wait):
        wl = [(pr.tm, tm_wait)]
        if elu_free[buf]:
            wl.append((pr.vm, elu_free[buf]))
        s1 = pr.op("S", lambda e, bank=bank, buf=buf: e.activation(
            out=elu_r[buf][:, :], in_=ps[bank][:, :], func=AF.Relu),
            waits=tuple(wl))
        v1 = pr.op("V", lambda e, bank=bank, buf=buf: e.tensor_sub(
            out=elu_m[buf][:, :], in0=ps[bank][:, :], in1=elu_r[buf][:, :]),
            waits=((pr.sm, s1),))
        s2 = pr.op("S", lambda e, buf=buf: e.activation(
            out=elu_m[buf][:, :], in_=elu_m[buf][:, :], func=AF.Exp),
            waits=((pr.vm, v1),))
        v2 = pr.op("V", lambda e, buf=buf, dst_ap_fn=dst_ap_fn: e.tensor_add(
            out=dst_ap_fn(), in0=elu_r[buf][:, :], in1=elu_m[buf][:, :]),
            waits=((pr.sm, s2),))
        elu_free[buf] = v2
        elu_s_last[0] = s2
        return v2

    a1_evac = []
    a1_mm = []
    q2_done = []
    vd_prev = 0
    xc_state = {"last": 0}
    pcount = 0
    xcb = [kv_z, xc_bf]

    def emit_ldsc(t):
        ld = pr.dma("Y", pr.xl[0], lambda e, t=t: e.dma_start(
            out=xtile[:, :], in_=x_d[t * P:(t + 1) * P, :]),
            waits=(((pr.sm, xc_state["last"]),) if xc_state["last"] else
                   ((pr.sm, u6), (pr.vm, u4))))
        wsc = [(pr.xl[0], ld), (pr.sm, u6)]
        if t >= 2:
            wsc.append((pr.tm, a1_mm[t - 2]))
        elif t == 1:
            wsc.append((pr.tm, w2_last_tm))
        sc = pr.op("S", lambda e, t=t: e.activation(
            out=xcb[t % 2][:, :], in_=xtile[:, :], func=AF.Identity,
            bias=nmean16[:, t:t + 1], scale=1.0),
            waits=tuple(wsc))
        xc_state["last"] = sc
        return ld, sc

    pref = {}
    for ts4 in range(4):
        for tloc in range(4):
            t = ts4 * 4 + tloc
            if t in pref:
                ld, sc = pref.pop(t)
            else:
                ld, sc = emit_ldsc(t)
            vd = pr.op("V", lambda e, t=t: e.tensor_scalar_mul(
                out=drstd[:, :], in0=ident[:, :], scalar1=rstd16[:, t:t + 1]),
                waits=((pr.sm, u6), (pr.cy, cy_all), (pr.vm, vd_prev)) +
                      (((pr.tm, a1_mm[t - 1]),) if t else ()))
            vd_prev = vd
            evl = {}
            for half in range(2):
                for i in range(half * 8, half * 8 + 8):
                    bank = 6 + (i // 4) % 2
                    wl = []
                    if i == half * 8:
                        wl = [(pr.vm, vd), (pr.sm, sc)]
                        if half == 0 and t >= 1:
                            wl.append((pr.vm, a1_evac[t - 1]))
                        if half == 0 and t == 0:
                            wl.append((pr.vm, vw2k))
                            wl.append((pr.vm, vw2q))
                        if half == 1:
                            wl.append((pr.vm, evl[0][0]))
                            wl.append((pr.sm, evl[0][1]))
                    pr.mm(lambda e, i=i, bank=bank, t=t: e.matmul(
                        ps[bank][:, (i % 4) * P:(i % 4 + 1) * P],
                        xcb[t % 2][:, i * P:(i + 1) * P], drstd[:, :],
                        start=(i % 4 == 0), stop=(i % 4 == 3)),
                        waits=tuple(wl), inc=(i % 4 == 3))
                tmv = pr.tm.n
                evv = pr.op("V", lambda e, half=half, t=t: e.tensor_copy(
                    out=xn3[:, half * 8:half * 8 + 4, t * P:(t + 1) * P],
                    in_=ps[6][:, :]),
                    waits=((pr.tm, tmv - 1),))
                evs = pr.op("S", lambda e, half=half, t=t: e.copy(
                    out=xn3[:, half * 8 + 4:half * 8 + 8, t * P:(t + 1) * P],
                    in_=ps[7][:, :]),
                    waits=((pr.tm, tmv),))
                evl[half] = (evv, evs)
            a1_mm.append(pr.tm.n)
            ev = pr.op("V", lambda e: e.memset(sq_t[:, :], 0.0),
                       waits=((pr.vm, evl[1][0]), (pr.sm, evl[1][1])))
            a1_evac.append(ev)
            wlg = [(pr.vm, ev)]
            if t == 0:
                wlg.append((pr.vm, w2ev_last))
                wlg.append((pr.cp, consts_done))
            for i in range(IC):
                t_lg = pr.mm(lambda e, i=i, t=t: e.matmul(
                    ps[5][:, t * E:(t + 1) * E],
                    xn[:, i * NTOK + t * P: i * NTOK + (t + 1) * P],
                    wg_bf[:, i * E:(i + 1) * E],
                    start=(t == 0 and i == 0), stop=(t == TT - 1 and i == IC - 1)),
                    waits=tuple(wlg) if i == 0 else (), inc=(i == IC - 1))
        for j in range(2):
            tp = (ts4 + 1) * 4 + j
            if tp < TT:
                pref[tp] = emit_ldsc(tp)
        for pack in range(8):
            bank = pcount % 4
            wl = []
            if pcount < 4:
                wl.append((pr.vm, vw2q))
                wl.append((pr.vm, a1_evac[ts4 * 4 + 3]))
            else:
                wl.append((pr.vm, q2_done[pcount - 4]))
                wl.append((pr.vm, a1_evac[ts4 * 4 + 3]))
            for i in range(IC):
                pr.mm(lambda e, i=i, pack=pack, ts4=ts4, bank=bank: e.matmul(
                    ps[bank][:, :],
                    w2q[:, i * 1024 + pack * P: i * 1024 + (pack + 1) * P],
                    xn[:, i * NTOK + ts4 * 512: i * NTOK + (ts4 + 1) * 512],
                    start=(i == 0), stop=False),
                    waits=tuple(wl) if i == 0 else ())
            tmv = pr.mm(lambda e, pack=pack, bank=bank: e.matmul(
                ps[bank][:, :], b2q_row[0:1, pack * P:(pack + 1) * P], ones[0:1, 0:512],
                start=False, stop=True), inc=True)
            dv = elu_tile(bank, (lambda pack=pack, ts4=ts4:
                                 q2[:, pack * NTOK + ts4 * 512: pack * NTOK + (ts4 + 1) * 512]),
                          pcount % 2, tmv)
            q2_done.append(dv)
            pcount += 1
    a1_s_last = pr.sm.n
    t_lg_last = t_lg
    if STOP == "a1":
        pr.quiesce()
        return pr
    v_pre = pr.op("V", lambda e: e.memset(g9bf[:, 8:9], 1.0))
    g_done = []
    gt_done = []
    tg1_list = []
    lg3 = lg_all.ap().rearrange("p (t e) -> p t e", e=E)
    mt3 = mtB.ap().rearrange("p (t e) -> p t e", e=E)
    ca3 = caB.ap().rearrange("p (t e) -> p t e", e=E)
    cb3 = cbB.ap().rearrange("p (t e) -> p t e", e=E)
    cc3g = ccB.ap().rearrange("p (t e) -> p t e", e=E)
    et3 = etB.ap().rearrange("p (t e) -> p t e", e=E)
    ga3 = g_all.ap().rearrange("p (t e) -> p t e", e=E)
    m16b = m16.ap().rearrange("p (t e) -> p t e", e=1).to_broadcast([P, TT, E])
    g16b = gate16.ap().rearrange("p (t e) -> p t e", e=1).to_broadcast([P, TT, E])
    gc0 = pr.op("V", lambda e: e.tensor_copy(out=lg_all[:, :], in_=ps[5][:, 0:TT * E]),
               waits=((pr.tm, t_lg_last), (pr.sm, a1_s_last)))
    gate_ps_read = gc0
    gc1 = pr.op("V", lambda e: e.reduce_max(out=m16[:, :], in_=lg3, axis=AX.X),
               waits=((pr.vm, gc0),))
    gc2 = pr.op("V", lambda e: e.tensor_tensor(out=mt3, in0=lg3, in1=m16b,
                                              op=ALU.is_equal),
               waits=((pr.vm, gc1),))
    gc3 = pr.op("V", lambda e: e.tensor_copy(out=caB[:, :], in_=mtB[:, :]),
               waits=((pr.vm, gc2),))
    gc4 = pr.op("V", lambda e: e.tensor_add(out=ca3[:, :, 1:E], in0=mt3[:, :, 1:E],
                                           in1=mt3[:, :, 0:E - 1]),
               waits=((pr.vm, gc3),))
    gc5 = pr.op("V", lambda e: e.tensor_copy(out=cbB[:, :], in_=caB[:, :]),
               waits=((pr.vm, gc4),))
    gc6 = pr.op("V", lambda e: e.tensor_add(out=cb3[:, :, 2:E], in0=ca3[:, :, 2:E],
                                           in1=ca3[:, :, 0:E - 2]),
               waits=((pr.vm, gc5),))
    gc7 = pr.op("V", lambda e: e.tensor_copy(out=ccB[:, :], in_=cbB[:, :]),
               waits=((pr.vm, gc6),))
    gc8 = pr.op("V", lambda e: e.tensor_add(out=cc3g[:, :, 4:E], in0=cb3[:, :, 4:E],
                                           in1=cb3[:, :, 0:E - 4]),
               waits=((pr.vm, gc7),))
    gc9 = pr.op("V", lambda e: e.tensor_scalar(out=ccB[:, :], in0=ccB[:, :],
                                              scalar1=1.0, scalar2=None,
                                              op0=ALU.is_equal),
               waits=((pr.vm, gc8),))
    gc10 = pr.op("V", lambda e: e.tensor_mul(out=mtB[:, :], in0=mtB[:, :],
                                            in1=ccB[:, :]),
                waits=((pr.vm, gc9),))
    gc11 = pr.op("V", lambda e: e.tensor_tensor(out=et3, in0=lg3, in1=m16b,
                                               op=ALU.subtract),
                waits=((pr.vm, gc2),))
    sB = pr.op("S", lambda e: e.activation(out=etB[:, :], in_=etB[:, :], func=AF.Exp),
               waits=((pr.vm, gc11),))
    gc12 = pr.op("V", lambda e: e.reduce_sum(out=ssum16[:, :], in_=et3, axis=AX.X),
                waits=((pr.sm, sB),))
    gc13 = pr.op("V", lambda e: e.reciprocal(out=gate16[:, :], in_=ssum16[:, :]),
                waits=((pr.vm, gc12),))
    gB = pr.op("V", lambda e: e.tensor_tensor(out=ga3, in0=mt3, in1=g16b,
                                              op=ALU.mult),
               waits=((pr.vm, gc13), (pr.vm, gc10)))
    def emit_gt(t, first_wait=()):
        bank = 6 + t % 2
        c15 = pr.op("V", lambda e, t=t: e.tensor_copy(out=g9bf[:, 0:E],
                                                      in_=g_all[:, t * E:(t + 1) * E]),
                    waits=((pr.vm, gB), (pr.vm, v_pre)) +
                          (((pr.tm, tg1_list[t - 1]),) if t else ()))
        tg1 = pr.mm(lambda e, bank=bank: e.matmul(
            ps[bank][0:9, 0:P], g9bf[:, 0:9], ident[:, :], start=True, stop=True),
            waits=((pr.vm, c15),) + tuple(first_wait) +
                  (((pr.vm, gt_done[t - 2]),) if t >= 2 else ()),
            inc=True)
        tg1_list.append(tg1)
        gv = pr.op("V", lambda e, t=t, bank=bank: e.tensor_copy(
            out=gT9[:, t * P:(t + 1) * P], in_=ps[bank][0:9, 0:P]),
            waits=((pr.tm, tg1),))
        g_done.append(gB)
        gt_done.append(gv)


    if STOP == "a1b":
        pr.quiesce()
        return pr



    k2_done = []
    kcount = 0
    k2_tm_last = [0]
    ks_first = [True]

    def emit_ksum_tile(tk):
        for packk in range(8):
            wlks = []
            if packk == 0:
                wlks.append((pr.vm, k2_done[2 * tk + 1]))
                if ks_first[0]:
                    wlks.append((pr.cy, cy_all))
                    ks_first[0] = False
            pr.mm(lambda e, tk=tk, packk=packk: e.matmul(
                ps[6][:, packk:packk + 1],
                k2[:, tk * 1024 + packk * P: tk * 1024 + (packk + 1) * P],
                onec[:, :],
                start=(tk == 0 and packk == 0), stop=(tk == TT - 1 and packk == 7)),
                waits=tuple(wlks), inc=(tk == TT - 1 and packk == 7))

    for t in range(TT):
        for half in range(2):
            bank = kcount % 4
            wl = []
            if kcount < 4:
                wl.append((pr.vm, vw2k))
                wl.append((pr.vm, q2_done[28 + kcount]))
            else:
                wl.append((pr.vm, k2_done[kcount - 4]))
            for i in range(IC):
                pr.mm(lambda e, i=i, t=t, half=half, bank=bank: e.matmul(
                    ps[bank][:, :],
                    xn[:, i * NTOK + t * P: i * NTOK + (t + 1) * P],
                    w2k[:, i * 1024 + half * 512: i * 1024 + (half + 1) * 512],
                    start=(i == 0), stop=False),
                    waits=tuple(wl) if i == 0 else ())
            tmv = pr.mm(lambda e, half=half, bank=bank: e.matmul(
                ps[bank][:, :], ones[0:1, 0:P], b2k_row[0:1, half * 512:(half + 1) * 512],
                start=False, stop=True), inc=True)
            k2_tm_last[0] = tmv
            dv = elu_tile(bank, (lambda t=t, half=half:
                                 k2[:, t * 1024 + half * 512: t * 1024 + (half + 1) * 512]),
                          kcount % 2, tmv)
            k2_done.append(dv)
            kcount += 1
        if t >= 1:
            emit_ksum_tile(t - 1)
    emit_ksum_tile(TT - 1)
    t_ksum = pr.tm.n
    v_ks = pr.op("V", lambda e: e.tensor_copy(out=kv_f32[:, 1024:1032], in_=ps[6][:, 0:8]),
                 waits=((pr.tm, t_ksum),))

    if STOP == "a2k":
        pr.quiesce()
        return pr
    attnf3 = attn_f.ap().rearrange("p (h t) -> p h t", t=1024)
    x4 = x_d.ap().rearrange("(t p) o -> p t o", p=P)
    o4 = out_d.ap().rearrange("(t p) o -> p t o", p=P)
    wo4 = woT_d.ap().rearrange("(i p) o -> p i o", p=P)
    wt4 = wT_d.ap().rearrange("e (i p) o -> e p i o", p=P)
    acc_free = [0, 0]
    wffn_free = [0, 0]
    ffn_idx = [0]
    first_wload = [True]
    pending_init = {}

    def emit_init(tgx, ohx):
        abx = (tgx * NOH + ohx) % 2
        wacc = []
        if acc_free[abx]:
            wacc.append((pr.asn[abx], acc_free[abx]))
        else:
            wacc.append((pr.tm, grp_last_mm[2 + abx]))
        return pr.dma("Y", pr.ai[abx], lambda e, tgx=tgx, ohx=ohx, abx=abx: e.dma_start(
            out=accs[abx].ap().rearrange("p (t o) -> p t o", o=512),
            in_=x4[:, tgx * 8:(tgx + 1) * 8, ohx * 512:(ohx + 1) * 512]),
            waits=tuple(wacc))

    def load_expert(ee, oh, extra_waits=(), extra_waits_b1=()):
        b0 = ffn_idx[0] % 2
        ffn_idx[0] += 1
        b1 = ffn_idx[0] % 2
        ffn_idx[0] += 1
        wl0 = list(extra_waits)
        if wffn_free[b0]:
            wl0.append((pr.tm, wffn_free[b0]))
        if first_wload[0]:
            wl0.append((pr.sm, a1_s_last))
            wl0.append((pr.vm, gB))
            first_wload[0] = False
        ld0 = pr.dma("P", pr.wf[b0], lambda e, ee=ee, oh=oh, b0=b0: e.dma_start(
            out=wffn[b0].ap().rearrange("p (i o) -> p i o", o=512),
            in_=wt4[ee, :, 0:8, oh * 512:(oh + 1) * 512]),
            waits=tuple(wl0))
        wl1 = list(extra_waits) + list(extra_waits_b1)
        if wffn_free[b1]:
            wl1.append((pr.tm, wffn_free[b1]))
        ld1 = pr.dma("P", pr.wf[b1], lambda e, ee=ee, oh=oh, b1=b1: e.dma_start(
            out=wffn[b1].ap().rearrange("p (i o) -> p i o", o=512),
            in_=wt4[ee, :, 8:16, oh * 512:(oh + 1) * 512]),
            waits=tuple(wl1))
        return (b0, ld0, b1, ld1)

    def emit_expert(tg, oh, ee, acc, prev_ev, ab, d_init, preloaded=None):
        if preloaded is None:
            b0, ld0, b1, ld1 = load_expert(ee, oh)
        else:
            b0, ld0, b1, ld1 = preloaded
        tv_tl = [0] * 8
        for tl in range(8):
            for i in range(8):
                wl5 = []
                if i == 0:
                    wl5.append((pr.vm, prev_ev[tl]))
                    wl5.append((pr.ai[ab], d_init))
                    if tl == 0:
                        wl5.append((pr.wf[b0], ld0))
                pr.mm(lambda e, i=i, tl=tl, b0=b0, tg=tg: e.matmul(
                    ps[tl][:, :],
                    xn[:, i * NTOK + (tg * 8 + tl) * P: i * NTOK + (tg * 8 + tl + 1) * P],
                    wffn[b0][:, (i % 8) * 512:(i % 8 + 1) * 512],
                    start=(i == 0), stop=False),
                    waits=tuple(wl5), inc=(tl == 7 and i == 7))
        wffn_free[b0] = pr.tm.n
        new_ev = []
        for tl in range(8):
            for i in range(8, IC):
                wl5 = []
                if i == 8 and tl == 0:
                    wl5.append((pr.wf[b1], ld1))
                v = pr.mm(lambda e, i=i, tl=tl, b1=b1, tg=tg: e.matmul(
                    ps[tl][:, :],
                    xn[:, i * NTOK + (tg * 8 + tl) * P: i * NTOK + (tg * 8 + tl + 1) * P],
                    wffn[b1][:, (i % 8) * 512:(i % 8 + 1) * 512],
                    start=False, stop=(i == IC - 1)),
                    waits=tuple(wl5), inc=(i == IC - 1))
                if i == IC - 1:
                    tv_tl[tl] = v
            ev = pr.op("V", lambda e, tl=tl, ee=ee, tg=tg, acc=acc: e.scalar_tensor_tensor(
                out=acc[:, tl * 512:(tl + 1) * 512],
                in0=ps[tl][:, :],
                scalar=g_all[:, (tg * 8 + tl) * E + ee: (tg * 8 + tl) * E + ee + 1],
                in1=acc[:, tl * 512:(tl + 1) * 512],
                op0=ALU.mult, op1=ALU.add),
                waits=((pr.tm, tv_tl[tl]), (pr.vm, g_done[tg * 8 + tl])))
            new_ev.append(ev)
        wffn_free[b1] = pr.tm.n
        return new_ev

    wv4 = wvT_d.ap().rearrange("(i p) o -> p i o", p=P)
    wvg3s = [w.ap().rearrange("p (i o) -> p i o", o=512) for w in wvg]
    kv_mm_last = None
    kv_grp_last = [0, 0, 0, 0]
    kv_ev_last = [0]
    hoist_lds = []
    vv_hist = []
    first_vv = True
    grp_last_mm = [0, 0, 0, 0]
    for grp in range(4):
        wb = grp % 2
        wvgrp = wvg[wb]
        ld = pr.dma("P", pr.wv[wb], lambda e, grp=grp, wb=wb: e.dma_start(
            out=wvg3s[wb][:, :, :], in_=wv4[:, :, grp * 512:(grp + 1) * 512]),
            waits=(((pr.tm, k2_tm_last[0]),) if grp < 2
                   else ((pr.tm, grp_last_mm[grp - 2]),)))
        if grp == 1:
            hb0 = ffn_idx[0] % 2
            ffn_idx[0] += 1
            hb1 = ffn_idx[0] % 2
            ffn_idx[0] += 1
            hld0 = pr.dma("P", pr.wf[hb0], lambda e, hb0=hb0: e.dma_start(
                out=wffn[hb0].ap().rearrange("p (i o) -> p i o", o=512),
                in_=wt4[0, :, 0:8, 0:512]),
                waits=((pr.sm, a1_s_last), (pr.vm, gB)))
            first_wload[0] = False
            hoist_lds.append((hb0, hld0, hb1))
        for t in range(TT):
            bank = t % 2
            wl = []
            if grp == 0 and t < 2:
                wl.append((pr.vm, k2_done[28 + t]))
            if t == 0:
                wl.append((pr.wv[wb], ld))
            if len(vv_hist) >= 2:
                wl.append((pr.vm, vv_hist[-2]))
            for i in range(IC):
                pr.mm(lambda e, i=i, t=t, bank=bank, wvgrp=wvgrp: e.matmul(
                    ps[bank][:, :],
                    xn[:, i * NTOK + t * P: i * NTOK + (t + 1) * P],
                    wvgrp[:, i * 512:(i + 1) * 512],
                    start=(i == 0), stop=False),
                    waits=tuple(wl) if i == 0 else ())
            tv = pr.mm(lambda e, grp=grp, bank=bank: e.matmul(
                ps[bank][:, :], ones[0:1, 0:P], bv_row[0:1, grp * 512:(grp + 1) * 512],
                start=False, stop=True), inc=True)
            grp_last_mm[grp] = tv
            wlv = [(pr.tm, tv)]
            if first_vv:
                wlv.append((pr.sm, elu_s_last[0]))
                first_vv = False
            vv = pr.op("V", lambda e, bank=bank: e.tensor_copy(
                out=asb[bank][:, 0:512], in_=ps[bank][:, :]), waits=tuple(wlv))
            vv_hist.append(vv)

            def emit_kv(tk):
                nonlocal kv_mm_last
                bk = tk % 2
                for local in range(2):
                    pk = grp * 2 + local
                    wlk = []
                    if local == 0:
                        wlk.append((pr.vm, vv_hist[-1 if tk == TT - 1 else -2]))
                    if grp == 2 and tk == 0 and local == 0:
                        wlk.append((pr.vm, v_ks))
                    kv_mm_last = pr.mm(lambda e, tk=tk, pk=pk, local=local, bk=bk,
                                       grp=grp: e.matmul(
                        ps[4 + grp][:, local * 256:(local + 1) * 256],
                        k2[:, tk * 1024 + pk * P: tk * 1024 + (pk + 1) * P],
                        asb[bk][:, local * 256:(local + 1) * 256],
                        start=(tk == 0 and local == 0),
                        stop=(tk == TT - 1 and local == 1)),
                        waits=tuple(wlk), inc=(local == 1))
                    if tk == TT - 1 and local == 1:
                        kv_grp_last[grp] = kv_mm_last

            if t > 0:
                emit_kv(t - 1)
            if t == TT - 1:
                emit_kv(t)
        for rh in range(2):
            src_ap = ps[4 + grp].ap().rearrange("p (l d) -> p l d", d=P)
            kv_ev_last[0] = pr.op("V", lambda e, g=grp, rh=rh, src_ap=src_ap: e.tensor_copy(
                out=kv_f32[rh * 64:(rh + 1) * 64, g * 256:(g + 1) * 256],
                in_=src_ap[rh * 64:(rh + 1) * 64, rh::2, :]),
                waits=((pr.tm, kv_grp_last[grp]),) if rh == 0 else ())
    v_kv1 = kv_ev_last[0]

    if STOP == "a3":
        pr.quiesce()
        return pr
    d_kv = pr.dma("P", pr.kva, lambda e: e.dma_start(out=cc_in[:, :], in_=kv_f32[:, :]),
                  waits=((pr.vm, v_kv1), (pr.vm, v_ks)))
    hb0, hld0, hb1 = hoist_lds[0]
    hld1 = pr.dma("P", pr.wf[hb1], lambda e, hb1=hb1: e.dma_start(
        out=wffn[hb1].ap().rearrange("p (i o) -> p i o", o=512),
        in_=wt4[0, :, 8:16, 0:512]),
        waits=((pr.kva, d_kv),))
    pr.emit("P", lambda e: e.collective_compute(
        "AllReduce", ALU.add,
        replica_groups=[[0, 1], [2, 3], [4, 5], [6, 7]],
        ins=[cc_in.ap().opt()], outs=[cc_out.ap().opt()]),
        waits=((pr.kva, d_kv),), incs=((pr.cm, 1),))
    pr.cm.n = 1
    z1 = pr.op("V", lambda e: e.memset(kv_z[:, :], 0.0))
    z2 = pr.op("V", lambda e: e.memset(ksum_z[:, :], 0.0), waits=((pr.vm, z1),))
    for t in range(TT):
        emit_gt(t, first_wait=(((pr.vm, v_kv1),) if t == 0 else ()))
    d_init00 = emit_init(0, 0)
    pending_init[(0, 0)] = d_init00
    hoist_pe = [z2] * 6 + [gt_done[14], gt_done[15]]
    hoist_pe = emit_expert(0, 0, 0, accs[0], hoist_pe, 0, d_init00,
                           preloaded=(hb0, hld0, hb1, hld1))
    hoist_pe = emit_expert(0, 0, 1, accs[0], hoist_pe, 0, d_init00)

    cc3 = cc_out.ap()[:, 0:1024].rearrange("p (hp d) -> p hp d", d=P)
    kvz3 = kv_z.ap().rearrange("p (hp d) -> p hp d", d=2 * P)
    pr.dma("P", pr.kvb, lambda e: e.dma_start(
        out=kvz3[0:64, :, 0:P], in_=cc3[0:64, 0:8, :]),
        waits=((pr.cm, 1), (pr.vm, z2)))
    pr.dma("P", pr.kvb, lambda e: e.dma_start(
        out=kvz3[64:128, :, P:2 * P], in_=cc3[64:128, 0:8, :]))
    ksz3 = ksum_z.ap().rearrange("p (hp c) -> p hp c", c=2)
    ko3 = cc_out.ap()[:, 1024:1032].rearrange("p (x c) -> p x c", c=1)
    pr.dma("P", pr.kvb, lambda e: e.dma_start(
        out=ksz3[0:64, :, 0:1], in_=ko3[0:64, :, :],
        allow_slow_non_contiguous=True))
    kvb_all = pr.dma("P", pr.kvb, lambda e: e.dma_start(
        out=ksz3[64:128, :, 1:2], in_=ko3[64:128, :, :],
        allow_slow_non_contiguous=True))

    if STOP == "cc":
        pr.quiesce()
        return pr
    cross_tg_guard = 0
    c2_seed = None
    for tg in range(NTG):
        if tg == 0:
            cross_tg_guard = hoist_pe[7]
            c2_seed = hoist_pe
        sgq_hist = []
        r1_par = [0, 0]
        r3_tl = [0] * 8
        scl_v = [0] * 8
        scl_s = [0] * 8
        ev_hist = []

        def c1_norm(tl):
            t = tg * 8 + tl
            nb = 0 if tl % 2 == 0 else 5
            wl = []
            if tl == 0 and tg == 0:
                wl.append((pr.kvb, kvb_all))
            if tl <= 1 and cross_tg_guard:
                wl.append((pr.vm, cross_tg_guard))
            if r1_par[tl % 2]:
                wl.append((pr.vm, r1_par[tl % 2]))
            for pack in range(8):
                pr.mm(lambda e, pack=pack, t=t, nb=nb: e.matmul(
                    ps[nb][:, pack * 2:(pack + 1) * 2],
                    q2[:, pack * NTOK + t * P: pack * NTOK + (t + 1) * P],
                    ksum_z[:, pack * 2:(pack + 1) * 2],
                    start=(pack == 0), stop=(pack == 7)),
                    waits=tuple(wl) if pack == 0 else (), inc=(pack == 7))
            t_nm = pr.tm.n
            rn = rnorm2[tl % 2]
            wr = [(pr.tm, t_nm)]
            if tl >= 2:
                wr.append((pr.sm, scl_s[tl - 2]))
                wr.append((pr.vm, scl_v[tl - 2]))
            r1 = pr.op("V", lambda e, nb=nb, rn=rn: e.tensor_copy(
                out=rn[:, :], in_=ps[nb][:, 0:H]), waits=tuple(wr))
            r2 = pr.op("V", lambda e, rn=rn: e.tensor_scalar(
                out=rn[:, :], in0=rn[:, :], scalar1=EPS_AT, scalar2=None,
                op0=ALU.add), waits=((pr.vm, r1),))
            r3 = pr.op("V", lambda e, rn=rn: e.reciprocal(out=rn[:, :], in_=rn[:, :]),
                       waits=((pr.vm, r2),))
            r1_par[tl % 2] = r1
            r3_tl[tl] = r3

        def c1_attn(tl):
            t = tg * 8 + tl
            rn = rnorm2[tl % 2]
            for gq in range(4):
                abank = 1 + gq % 2
                wl2 = [(pr.vm, r3_tl[tl])]
                if len(sgq_hist) >= 2:
                    pv, psv = sgq_hist[-2]
                    wl2.append((pr.vm, pv))
                    wl2.append((pr.sm, psv))
                for lp in range(2):
                    pack = gq * 2 + lp
                    pr.mm(lambda e, lp=lp, pack=pack, t=t, abank=abank: e.matmul(
                        ps[abank][:, lp * 256:(lp + 1) * 256],
                        q2[:, pack * NTOK + t * P: pack * NTOK + (t + 1) * P],
                        kv_z[:, pack * 256:(pack + 1) * 256],
                        start=(lp == 0), stop=(lp == 1)),
                        waits=tuple(wl2) if lp == 0 else (), inc=(lp == 1))
                t_at = pr.tm.n
                if gq % 2 == 0:
                    sc = 0
                    for j in range(4):
                        h = gq * 4 + j
                        sc = pr.op("V", lambda e, h=h, j=j, abank=abank, tl=tl, rn=rn: e.tensor_scalar_mul(
                            out=asb[tl % 2][:, h * P:(h + 1) * P],
                            in0=ps[abank][:, j * P:(j + 1) * P],
                            scalar1=rn[:, h:h + 1]),
                            waits=((pr.tm, t_at),) if j == 0 else ())
                    sgq_hist.append((sc, 0))
                    scl_v[tl] = sc
                else:
                    sc = 0
                    for j in range(4):
                        h = gq * 4 + j
                        sc = pr.op("S", lambda e, h=h, j=j, abank=abank, tl=tl, rn=rn: e.activation(
                            out=asb[tl % 2][:, h * P:(h + 1) * P],
                            in_=ps[abank][:, j * P:(j + 1) * P],
                            func=AF.Copy, scale=rn[:, h:h + 1]),
                            waits=((pr.tm, t_at),) if j == 0 else ())
                    sgq_hist.append((0, sc))
                    scl_s[tl] = sc

        def c1_transp(tl):
            for gq in range(4):
                tbank = 3 + gq % 2
                wl3 = [(pr.vm, scl_v[tl]), (pr.sm, scl_s[tl])]
                if len(ev_hist) >= 2:
                    wl3.append((pr.vm, ev_hist[-2]))
                for j in range(4):
                    h = gq * 4 + j
                    pr.mm(lambda e, h=h, j=j, tbank=tbank, tl=tl: e.matmul(
                        ps[tbank][:, j * P:(j + 1) * P],
                        asb[tl % 2][:, h * P:(h + 1) * P], ident[:, :],
                        start=(j == 0), stop=(j == 3)),
                        waits=tuple(wl3) if j == 0 else (), inc=(j == 3))
                t_tr = pr.tm.n
                ev = pr.op("V", lambda e, gq=gq, tbank=tbank, tl=tl: e.tensor_copy(
                    out=attnf3[:, gq * 4:(gq + 1) * 4, tl * P:(tl + 1) * P],
                    in_=ps[tbank].ap().rearrange("p (j d) -> p j d", d=P)),
                    waits=((pr.tm, t_tr),))
                ev_hist.append(ev)

        c1_norm(0)
        c1_norm(1)
        c1_attn(0)
        for tl in range(1, 8):
            if tl + 1 < 8:
                c1_norm(tl + 1)
            c1_attn(tl)
            c1_transp(tl - 1)
        c1_transp(7)
        c1_last = pr.vm.n
        if STOP in ("c1", "c1n", "c1a"):
            pr.quiesce()
            return pr

        last_ps_reader = [c1_last] * 8

        for oh in range(NOH):
            ab = (tg * NOH + oh) % 2
            acc = accs[ab]
            key = (tg, oh)
            if key in pending_init:
                d_init = pending_init.pop(key)
            else:
                d_init = emit_init(tg, oh)
            ohalf = []
            for halfw in range(2):
                b = ffn_idx[0] % 2
                ffn_idx[0] += 1
                wlw = []
                if wffn_free[b]:
                    wlw.append((pr.tm, wffn_free[b]))
                ld = pr.dma("P", pr.wf[b], lambda e, halfw=halfw, oh=oh, b=b: e.dma_start(
                    out=wffn[b].ap().rearrange("p (i o) -> p i o", o=512),
                    in_=wo4[:, halfw * 8:(halfw + 1) * 8, oh * 512:(oh + 1) * 512]),
                    waits=tuple(wlw))
                ohalf.append((b, ld))
            for idx, (b, ld) in enumerate(ohalf):
                for tl in range(8):
                    for ii in range(8):
                        i = idx * 8 + ii
                        wl4 = []
                        if i == 0:
                            wl4 += [(pr.vm, last_ps_reader[tl]), (pr.ai[ab], d_init)]
                        if ii == 0 and tl == 0:
                            wl4.append((pr.wf[b], ld))
                        pr.mm(lambda e, i=i, ii=ii, tl=tl, b=b, tg=tg: e.matmul(
                            ps[tl][:, :],
                            attn_f[:, i * 1024 + tl * P: i * 1024 + (tl + 1) * P],
                            wffn[b][:, ii * 512:(ii + 1) * 512],
                            start=(i == 0), stop=False),
                            waits=tuple(wl4),
                            inc=(tl == 7 and ii == 7))
                wffn_free[b] = pr.tm.n
            prev_ev = []
            for tl in range(8):
                tv = pr.mm(lambda e, tl=tl, tg=tg, oh=oh: e.matmul(
                    ps[tl][:, :], gT9[:, (tg * 8 + tl) * P:(tg * 8 + tl + 1) * P],
                    b9[0:9, oh * 512:(oh + 1) * 512],
                    start=False, stop=True),
                    waits=((pr.vm, gt_done[tg * 8 + tl]), (pr.cp, consts_done)),
                    inc=True)
                ev = pr.op("V", lambda e, tl=tl, acc=acc: e.tensor_add(
                    out=acc[:, tl * 512:(tl + 1) * 512],
                    in0=ps[tl][:, :], in1=acc[:, tl * 512:(tl + 1) * 512]),
                    waits=((pr.tm, tv),))
                prev_ev.append(ev)
            e_start = 2 if (tg == 0 and oh == 0) else 0
            if e_start == 2:
                prev_ev = [max(a, b) for a, b in zip(prev_ev, c2_seed)]
            for ee in range(e_start, E):
                prev_ev = emit_expert(tg, oh, ee, acc, prev_ev, ab, d_init)
            last_ps_reader = [v for v in prev_ev]
            seq = [(tgx, ohx) for tgx in range(NTG) for ohx in range(NOH)]
            pos = seq.index((tg, oh))
            if pos + 1 < len(seq):
                nxt = seq[pos + 1]
                pending_init[nxt] = emit_init(*nxt)
            st = pr.dma("Y", pr.asn[ab], lambda e, tg=tg, oh=oh, acc=acc: e.dma_start(
                out=o4[:, tg * 8:(tg + 1) * 8, oh * 512:(oh + 1) * 512],
                in_=acc.ap().rearrange("p (t o) -> p t o", o=512)),
                waits=tuple((pr.vm, v) for v in prev_ev))
            acc_free[ab] = st
            cross_tg_guard = last_ps_reader[7]

    f0, f1 = pr.asn[0].n, pr.asn[1].n
    pr.emit("Y", lambda e, f0=f0: e.wait_ge(pr.asn[0].h, f0))
    pr.emit("Y", lambda e, f1=f1: e.wait_ge(pr.asn[1].h, f1))
    return pr


def _build_nc():
    nc = bass.Bass("TRN2", target_bir_lowering=False)
    pr = build(nc)
    body = pr.realize()
    with nc.Block() as block:
        block.tensor(body("T"))
        block.vector(body("V"))
        block.scalar(body("S"))
        block.gpsimd(body("P"))
        block.sync(body("Y"))
    return nc


def kernel(L0_attn_hidden_states, L0_attn_wq, L0_attn_bq, L0_attn_wk, L0_attn_bk,
           L0_attn_wv, L0_attn_bv, L0_attn_wo, L0_attn_bo, L0_attn_wmap_q,
           L0_attn_wmap_k, L0_ffn_wg, L0_ffn_W, L0_ffn_b):
    x = np.ascontiguousarray(np.asarray(L0_attn_hidden_states, dtype=np.float32))
    wq = np.ascontiguousarray(np.asarray(L0_attn_wq, dtype=np.float32))
    wk = np.ascontiguousarray(np.asarray(L0_attn_wk, dtype=np.float32))
    wvT = np.ascontiguousarray(np.asarray(L0_attn_wv, dtype=np.float32).T)
    woT = np.ascontiguousarray(np.asarray(L0_attn_wo, dtype=np.float32).T)
    wT = np.ascontiguousarray(np.asarray(L0_ffn_W, dtype=np.float32).transpose(0, 2, 1))
    wmapqT = np.ascontiguousarray(np.asarray(L0_attn_wmap_q, dtype=np.float32).T)
    wmapkT = np.ascontiguousarray(np.asarray(L0_attn_wmap_k, dtype=np.float32).T)
    wg = np.ascontiguousarray(np.asarray(L0_ffn_wg, dtype=np.float32))
    be = np.ascontiguousarray(np.asarray(L0_ffn_b, dtype=np.float32))
    bq = np.ascontiguousarray(np.asarray(L0_attn_bq, dtype=np.float32).reshape(1, D))
    bk = np.ascontiguousarray(np.asarray(L0_attn_bk, dtype=np.float32).reshape(1, D))
    bv = np.ascontiguousarray(np.asarray(L0_attn_bv, dtype=np.float32).reshape(1, D))
    bo = np.ascontiguousarray(np.asarray(L0_attn_bo, dtype=np.float32).reshape(1, D))

    nc = _build_nc()
    in_maps = []
    for c in range(8):
        b, half = c // 2, c % 2
        xs = np.ascontiguousarray(x[b, half * NTOK:(half + 1) * NTOK, :])
        in_maps.append({
            "x": xs, "wq": wq, "wk": wk, "wvT": wvT, "woT": woT, "wT": wT,
            "wmapqT": wmapqT, "wmapkT": wmapkT, "wg": wg,
            "bq": bq, "bk": bk, "bv": bv, "bo": bo, "be": be,
        })
    res = run_bass_kernel_spmd(nc, in_maps, core_ids=list(range(8)))
    out = np.empty((4, 4096, D), dtype=np.float32)
    for c in range(8):
        b, half = c // 2, c % 2
        out[b, half * NTOK:(half + 1) * NTOK, :] = res.results[c]["out"]
    return out


# version 19
# speedup vs baseline: 1.0061x; 1.0061x over previous
import numpy as np
import ml_dtypes
import concourse.bass as bass
import concourse.mybir as mybir
from concourse.bass_utils import run_bass_kernel_spmd

F32 = mybir.dt.float32
BF16 = mybir.dt.bfloat16
AF = mybir.ActivationFunctionType
ALU = mybir.AluOpType
AX = mybir.AxisListType

P = 128
TT = 16
IC = 16
H = 16
DIM = 64
E = 8
NTOK = 2048
D = 2048
NTG = 2
NOH = 4
EPS_LN = 1e-5
EPS_AT = 1e-6
STOP = "full"


class Sem:
    def __init__(self, nc, name):
        self.h = nc.alloc_semaphore(name)
        self.n = 0

    def inc(self, k=1):
        self.n += k
        return self.n


class Prog:
    ENGS = ("T", "V", "S", "P", "Y")

    def __init__(self, nc):
        self.nc = nc
        self.streams = {e: [] for e in self.ENGS}
        self.tm = Sem(nc, "tm")
        self.vm = Sem(nc, "vm")
        self.sm = Sem(nc, "sm")
        self.cm = Sem(nc, "cm")
        self.cy = Sem(nc, "cy")
        self.cp = Sem(nc, "cp")
        self.xl = [Sem(nc, "xl0"), Sem(nc, "xl1")]
        self.ws = [Sem(nc, "ws0"), Sem(nc, "ws1")]
        self.pm = Sem(nc, "pm")
        self.yk = [Sem(nc, "yk0"), Sem(nc, "yk1")]
        self.wv = [Sem(nc, "wv0"), Sem(nc, "wv1")]
        self.kva = Sem(nc, "kva")
        self.kvb = Sem(nc, "kvb")
        self.wf = [Sem(nc, "wf0"), Sem(nc, "wf1")]
        self.ai = [Sem(nc, "ai0"), Sem(nc, "ai1")]
        self.asn = [Sem(nc, "as0"), Sem(nc, "as1")]

    def emit(self, eng, fn, waits=(), incs=()):
        ws = tuple((s, v) for (s, v) in waits if v)
        self.streams[eng].append((ws, fn, tuple(incs)))

    def op(self, eng, fn, waits=()):
        sem = {"V": self.vm, "S": self.sm}[eng]
        v = sem.inc(1)
        self.emit(eng, fn, waits, ((sem, 1),))
        return v

    def mm(self, fn, waits=(), inc=False):
        if inc:
            v = self.tm.inc(1)
            self.emit("T", fn, waits, ((self.tm, 1),))
            return v
        self.emit("T", fn, waits, ())
        return None

    def dma(self, q, sem, fn, waits=()):
        v = sem.inc(16)
        self.emit(q, fn, waits, ((sem, 16),))
        return v

    def quiesce(self):
        tails = [(self.tm, self.tm.n), (self.vm, self.vm.n), (self.sm, self.sm.n),
                 (self.cy, self.cy.n), (self.cp, self.cp.n),
                 (self.ws[0], self.ws[0].n), (self.ws[1], self.ws[1].n),
                 (self.pm, self.pm.n),
                 (self.yk[0], self.yk[0].n), (self.yk[1], self.yk[1].n),
                 (self.xl[0], self.xl[0].n), (self.xl[1], self.xl[1].n),
                 (self.wv[0], self.wv[0].n), (self.wv[1], self.wv[1].n),
                 (self.kva, self.kva.n), (self.kvb, self.kvb.n),
                 (self.cm, self.cm.n),
                 (self.wf[0], self.wf[0].n), (self.wf[1], self.wf[1].n),
                 (self.ai[0], self.ai[0].n), (self.ai[1], self.ai[1].n),
                 (self.asn[0], self.asn[0].n), (self.asn[1], self.asn[1].n)]
        for s, v in tails:
            if v:
                self.emit("Y", lambda e, s=s, v=v: e.wait_ge(s.h, v))

    def realize(self):
        def body(eng_key):
            stream = self.streams[eng_key]

            def _(e):
                for waits, fn, incs in stream:
                    for s, v in waits:
                        e.wait_ge(s.h, v)
                    inst = fn(e)
                    for s, k in incs:
                        inst.then_inc(s.h, k)
            return _
        return body


def build(nc):
    x_d = nc.dram_tensor("x", [NTOK, D], F32, kind="ExternalInput")
    wq_d = nc.dram_tensor("wq", [D, D], F32, kind="ExternalInput")
    wk_d = nc.dram_tensor("wk", [D, D], F32, kind="ExternalInput")
    wvT_d = nc.dram_tensor("wvT", [D, D], F32, kind="ExternalInput")
    woT_d = nc.dram_tensor("woT", [D, D], F32, kind="ExternalInput")
    wT_d = nc.dram_tensor("wT", [E, D, D], F32, kind="ExternalInput")
    wmapqT_d = nc.dram_tensor("wmapqT", [P, DIM], F32, kind="ExternalInput")
    wmapkT_d = nc.dram_tensor("wmapkT", [P, DIM], F32, kind="ExternalInput")
    wg_d = nc.dram_tensor("wg", [D, E], F32, kind="ExternalInput")
    bq_d = nc.dram_tensor("bq", [1, D], F32, kind="ExternalInput")
    bk_d = nc.dram_tensor("bk", [1, D], F32, kind="ExternalInput")
    bv_d = nc.dram_tensor("bv", [1, D], F32, kind="ExternalInput")
    bo_d = nc.dram_tensor("bo", [1, D], F32, kind="ExternalInput")
    be_d = nc.dram_tensor("be", [E, D], F32, kind="ExternalInput")
    out_d = nc.dram_tensor("out", [NTOK, D], F32, kind="ExternalOutput")

    cc_in = nc.dram_tensor("cc_in", [P, 1032], F32, kind="Internal")
    cc_out = nc.dram_tensor("cc_out", [P, 1032], F32, kind="Internal")

    ident_d = nc.inline_tensor(np.eye(P, dtype=ml_dtypes.bfloat16), name="identbf")
    ones_d = nc.inline_tensor(np.ones((1, 512), dtype=ml_dtypes.bfloat16), name="onesbf")
    onec_d = nc.inline_tensor(np.ones((P, 1), dtype=ml_dtypes.bfloat16), name="onecbf")

    A = nc.alloc_sbuf_tensor
    AT = nc.alloc_sbuf_tensor_at
    addr = lambda h: nc.lookup_mloc(h).addr

    xn = A("xn", [P, IC * NTOK], BF16)
    q2 = A("q2", [P, 8 * NTOK], BF16)
    rc = A("rc", [P, IC * 1024], BF16)
    w2q = AT("w2q", [P, IC * 1024], BF16, offset=addr(rc))
    k2 = AT("k2", [P, TT * 1024], BF16, offset=addr(rc))
    attn_f = AT("attn_f", [P, H * 1024], BF16, offset=addr(rc))
    rd = A("rd", [P, IC * 1024], BF16)
    w2k = AT("w2k", [P, IC * 1024], BF16, offset=addr(rd))
    wvg = [AT("wvg0", [P, IC * 512], BF16, offset=addr(rd)),
           AT("wvg1", [P, IC * 512], BF16, offset=addr(rd) + IC * 512 * 2)]
    acc0 = AT("acc0", [P, 8 * 512], F32, offset=addr(rd))
    acc1 = AT("acc1", [P, 8 * 512], F32, offset=addr(rd) + 8 * 512 * 4)
    accs = [acc0, acc1]
    re = A("re", [P, 8192], BF16)
    xtile = AT("xtile", [P, D], F32, offset=addr(re))
    wffn = [AT("wffn0", [P, 4096], BF16, offset=addr(re)),
            AT("wffn1", [P, 4096], BF16, offset=addr(re) + 4096 * 2)]
    wst = [AT("wst0", [P, D], BF16, offset=addr(re) + 8192),
           AT("wst1", [P, D], BF16, offset=addr(re) + 12288)]
    ascr = A("ascr", [P, 4096], BF16)
    asb = [AT("asb0", [P, D], BF16, offset=addr(ascr)),
           AT("asb1", [P, D], BF16, offset=addr(ascr) + 4096)]
    elu_r = [AT("elu_r0", [P, 512], F32, offset=addr(ascr)),
             AT("elu_r1", [P, 512], F32, offset=addr(ascr) + 4096)]
    elu_m = [AT("elu_m0", [P, 512], F32, offset=addr(ascr) + 2048),
             AT("elu_m1", [P, 512], F32, offset=addr(ascr) + 4096 + 2048)]
    xtile1 = AT("xtile1", [P, D], F32, offset=addr(ascr))
    xts = [None, None]

    ident = A("ident", [P, P], BF16)
    ones = A("ones", [1, 512], BF16)
    onec = A("onec", [P, 1], BF16)
    wg_bf = A("wg_bf", [P, IC * E], BF16)
    wmq = A("wmq", [P, DIM], BF16)
    wmk = A("wmk", [P, DIM], BF16)
    bq_row = AT("bq_row", [1, D], BF16, offset=addr(rc))
    bk_row = AT("bk_row", [1, D], BF16, offset=addr(rc) + 4096)
    bv_row = A("bv_row", [1, D], BF16)
    b9 = A("b9", [9, D], BF16)
    bq_col = A("bq_col", [P, H], BF16)
    bk_col = A("bk_col", [P, H], BF16)
    b2q_row = A("b2q_row", [1, 1024], BF16)
    b2k_row = A("b2k_row", [1, 1024], BF16)
    g_all = A("g_all", [P, TT * E], F32)
    lg_all = AT("lg_all", [P, TT * E], F32, offset=addr(re))
    mtB = AT("mtB", [P, TT * E], F32, offset=addr(re) + 512)
    caB = AT("caB", [P, TT * E], F32, offset=addr(re) + 1024)
    cbB = AT("cbB", [P, TT * E], F32, offset=addr(re) + 1536)
    ccB = AT("ccB", [P, TT * E], F32, offset=addr(re) + 2048)
    etB = AT("etB", [P, TT * E], F32, offset=addr(re) + 2560)
    m16 = A("m16", [P, TT], F32)
    ssum16 = A("ssum16", [P, TT], F32)
    gate16 = A("gate16", [P, TT], F32)
    g9bf = A("g9bf", [P, 9], BF16)
    gT9 = A("gT9", [9, TT * P], BF16)
    kv_f32 = AT("kv_f32", [P, 1032], F32, offset=addr(re) + 8192)
    kv_z = A("kv_z", [P, 2048], BF16)
    ksum_z = A("ksum_z", [P, 2 * E], BF16)
    nmean = A("nmean", [P, 1], F32)
    nsum16 = A("nsum16", [P, TT], F32)
    sumsq16 = A("sumsq16", [P, TT], F32)
    nmean16 = A("nmean16", [P, TT], F32)
    rstd16 = A("rstd16", [P, TT], F32)
    vtmp16 = A("vtmp16", [P, TT], F32)
    sumsq = A("sumsq", [P, 1], F32)
    nsum = A("nsum", [P, 1], F32)
    varv = A("varv", [P, 1], F32)
    sq_t = A("sq_t", [P, 1], F32)
    rstd = A("rstd", [P, 1], F32)
    drstd = A("drstd", [P, P], BF16)
    xc_bf = AT("xc_bf", [P, D], BF16, offset=addr(re) + 8192)
    lg = A("lg", [P, E], F32)
    mxv = A("mxv", [P, 1], F32)
    nmx = A("nmx", [P, 1], F32)
    mt_t = A("mt_t", [P, E], F32)
    ca_t = A("ca_t", [P, E], F32)
    cb_t = A("cb_t", [P, E], F32)
    cc_t = A("cc_t", [P, E], F32)
    et_t = A("et_t", [P, E], F32)
    ssum = A("ssum", [P, 1], F32)
    gate = A("gate", [P, 1], F32)
    rnorm2 = [A("rnorm0", [P, H], F32), A("rnorm1", [P, H], F32)]
    sq_trash2 = [AT("sq_trash0", [P, D], BF16, offset=addr(q2) + 4096),
                 AT("sq_trash1", [P, D], BF16, offset=addr(q2) + 8192)]
    wkst = [AT("wkst0", [P, D], F32, offset=addr(q2) + 12288),
            AT("wkst1", [P, D], F32, offset=addr(q2) + 20480)]

    ps = [nc.alloc_psum_tensor(f"ps{k}", [P, 512], F32) for k in range(8)]

    pr = Prog(nc)
    xn3 = xn.ap().rearrange("p (i t) -> p i t", t=NTOK)

    pr.dma("Y", pr.cy, lambda e: e.dma_start(out=ident[:, :], in_=ident_d[:, :]))
    pr.dma("Y", pr.cy, lambda e: e.dma_start(out=ones[:, :], in_=ones_d[:, :]))
    cy_all = pr.dma("Y", pr.cy, lambda e: e.dma_start(out=onec[:, :], in_=onec_d[:, :]))
    wg_ap = wg_d.ap().rearrange("(c p) e -> p c e", p=P)
    wg_bf3 = wg_bf.ap().rearrange("p (c e) -> p c e", e=E)
    pr.dma("P", pr.cp, lambda e: e.dma_start(out=wg_bf3, in_=wg_ap))
    pr.dma("P", pr.cp, lambda e: e.dma_start(out=wmq[:, :], in_=wmapqT_d[:, :]))
    pr.dma("P", pr.cp, lambda e: e.dma_start(out=wmk[:, :], in_=wmapkT_d[:, :]))
    pr.dma("P", pr.cp, lambda e: e.dma_start(out=bq_row[:, :], in_=bq_d[:, :]))
    pr.dma("P", pr.cp, lambda e: e.dma_start(out=bk_row[:, :], in_=bk_d[:, :]))
    pr.dma("P", pr.cp, lambda e: e.dma_start(out=bv_row[:, :], in_=bv_d[:, :]))
    pr.dma("P", pr.cp, lambda e: e.dma_start(out=b9[0:E, :], in_=be_d[:, :]))
    consts_done = pr.dma("P", pr.cp, lambda e: e.dma_start(out=b9[E:E + 1, :], in_=bo_d[:, :]))

    def bias_cols(row, col_sb, wait0):
        tv = None
        for h in range(H):
            tv = pr.mm(lambda e, h=h, row=row: e.matmul(
                ps[0][:, h:h + 1], row[0:1, h * P:(h + 1) * P], ones[0:1, 0:1],
                start=(h == 0), stop=(h == H - 1)),
                waits=wait0 if h == 0 else (), inc=(h == H - 1))
        return pr.op("V", lambda e, col_sb=col_sb: e.tensor_copy(
            out=col_sb[:, :], in_=ps[0][:, 0:H]), waits=((pr.tm, tv),))

    v_bqc = bias_cols(bq_row, bq_col,
                      ((pr.cp, consts_done), (pr.cy, cy_all)))
    v_bkc = bias_cols(bk_row, bk_col, ((pr.vm, v_bqc),))

    def b2_build(col_sb, wmap_sb, row_out, wait_v):
        outv = None
        for half in range(2):
            tv = None
            for hh in range(8):
                h = half * 8 + hh
                wl = []
                if half == 0 and hh == 0:
                    wl.append((pr.vm, wait_v))
                if half == 1 and hh == 0:
                    wl.append((pr.vm, outv))
                tv = pr.mm(lambda e, h=h, hh=hh, col_sb=col_sb, wmap_sb=wmap_sb: e.matmul(
                    ps[1][0:1, hh * DIM:(hh + 1) * DIM],
                    col_sb[:, h:h + 1], wmap_sb[:, :],
                    start=(hh == 0), stop=(hh == 7)),
                    waits=tuple(wl), inc=(hh == 7))
            outv = pr.op("V", lambda e, half=half, row_out=row_out: e.tensor_copy(
                out=row_out[0:1, half * 512:(half + 1) * 512], in_=ps[1][0:1, 0:512]),
                waits=((pr.tm, tv),))
        return outv

    v_b2q = b2_build(bq_col, wmq, b2q_row, v_bqc)
    v_b2k = b2_build(bk_col, wmk, b2k_row, v_b2q)

    wst_free = [0, 0]
    stage_idx = [0]
    w2q3 = w2q.ap().rearrange("p (i m) -> p i m", m=1024)
    w2k3 = w2k.ap().rearrange("p (i m) -> p i m", m=1024)
    w2_evs = {"q": [], "k": []}

    pool_cast_last = [0, 0]
    w2_all_evs = []

    def emit_w2_chunk(which, h, ev_extra=0):
        src_d, wmap_sb, dst3 = ((wq_d, wmq, w2q3) if which == "q"
                                else (wk_d, wmk, w2k3))
        evs = w2_evs[which]
        b = stage_idx[0] % 2
        stage_idx[0] += 1
        if which == "q":
            ld = pr.dma("P", pr.ws[b], lambda e, h=h, b=b, src_d=src_d: e.dma_start(
                out=wst[b][:, :], in_=src_d[h * P:(h + 1) * P, :]),
                waits=(((pr.vm, wst_free[b]),) if wst_free[b] else ()))
            mm_wait = (pr.ws[b], ld)
        else:
            kb = h % 2
            wl_ld = []
            if pool_cast_last[kb]:
                wl_ld.append(pool_cast_last[kb])
            ldf = pr.dma("S", pr.yk[kb], lambda e, h=h, kb=kb, src_d=src_d: e.dma_start(
                out=wkst[kb][:, :], in_=src_d[h * P:(h + 1) * P, :]),
                waits=tuple(wl_ld))
            cwl = [(pr.yk[kb], ldf)]
            if wst_free[b]:
                cwl.append((pr.vm, wst_free[b]))
            if kb == 0:
                cv = pr.op("V", lambda e, b=b, kb=kb: e.tensor_copy(
                    out=wst[b][:, :], in_=wkst[kb][:, :]), waits=tuple(cwl))
                mm_wait = (pr.vm, cv)
                pool_cast_last[kb] = (pr.vm, cv)
            else:
                cv = pr.op("S", lambda e, b=b, kb=kb: e.copy(
                    out=wst[b][:, :], in_=wkst[kb][:, :]), waits=tuple(cwl))
                mm_wait = (pr.sm, cv)
                pool_cast_last[kb] = (pr.sm, cv)
        allv = w2_all_evs
        for i in range(IC):
            bank = 4 + (h % 2) * 2 + i // 8
            wl = []
            if i == 0:
                wl.append(mm_wait)
                wl.append((pr.cp, consts_done))
                if allv:
                    wl.append((pr.vm, allv[-1]))
                else:
                    wl.append((pr.vm, v_b2k))
            pr.mm(lambda e, i=i, b=b, bank=bank, wmap_sb=wmap_sb: e.matmul(
                ps[bank][:, (i % 8) * DIM:(i % 8 + 1) * DIM],
                wst[b][:, i * P:(i + 1) * P], wmap_sb[:, :],
                start=(i % 8 == 0), stop=(i % 8 == 7)),
                waits=tuple(wl), inc=(i == 7 or i == IC - 1))
        tmv = pr.tm.n
        evl = None
        for half in range(2):
            bank = 4 + (h % 2) * 2 + half
            wle = [(pr.tm, tmv - (1 - half))]
            if ev_extra and half == 0:
                wle.append((pr.sm, ev_extra))
            evl = pr.op("V", lambda e, h=h, half=half, bank=bank, dst3=dst3: e.tensor_copy(
                out=dst3[:, half * 8:(half + 1) * 8, h * DIM:(h + 1) * DIM],
                in_=ps[bank].ap().rearrange("p (i m) -> p i m", m=DIM)),
                waits=tuple(wle))
        evs.append(evl)
        w2_all_evs.append(evl)
        wst_free[b] = evl

    xts[0], xts[1] = xtile, xtile1
    xl_v = [0, 0]
    xl_s = [0, 0]
    for t in range(TT):
        xt = xts[t % 2]
        ld = pr.dma("Y", pr.xl[t % 2], lambda e, t=t, xt=xt: e.dma_start(
            out=xt[:, :], in_=x_d[t * P:(t + 1) * P, :]),
            waits=((pr.vm, xl_v[t % 2]), (pr.sm, xl_s[t % 2])))
        v1 = pr.op("V", lambda e, t=t, xt=xt: e.reduce_sum(
            out=nsum16[:, t:t + 1], in_=xt[:, :], axis=AX.X, negate=True),
            waits=((pr.xl[t % 2], ld),))
        s1 = pr.op("S", lambda e, t=t, xt=xt: e.activation(
            out=sq_trash2[t % 2][:, :], in_=xt[:, :], func=AF.Square,
            accum_out=sumsq16[:, t:t + 1]),
            waits=((pr.xl[t % 2], ld), (pr.sm, xl_s[t % 2])))
        xl_v[t % 2], xl_s[t % 2] = v1, s1
        emit_w2_chunk("q", t)
        emit_w2_chunk("k", t)
    vw2q = w2_evs["q"][-1]
    vw2k = w2_evs["k"][-1]
    w2ev_last = w2_all_evs[-1]
    w2_last_tm = pr.tm.n
    st_v = pr.vm.n
    st_s = pr.sm.n
    u1 = pr.op("V", lambda e: e.tensor_scalar_mul(out=nmean16[:, :], in0=nsum16[:, :],
                                                  scalar1=1.0 / D),
               waits=((pr.vm, st_v),))
    u2 = pr.op("V", lambda e: e.tensor_scalar(out=vtmp16[:, :], in0=nmean16[:, :],
                                              scalar1=nmean16[:, :] if False else None,
                                              scalar2=None, op0=ALU.mult)
               if False else e.tensor_mul(out=vtmp16[:, :], in0=nmean16[:, :],
                                          in1=nmean16[:, :]),
               waits=((pr.vm, u1),))
    u3 = pr.op("V", lambda e: e.tensor_scalar(out=vtmp16[:, :], in0=vtmp16[:, :],
                                              scalar1=EPS_LN, scalar2=None,
                                              op0=ALU.subtract),
               waits=((pr.vm, u2),))
    u4 = pr.op("V", lambda e: e.scalar_tensor_tensor(
        out=vtmp16[:, :], in0=sumsq16[:, :], scalar=1.0 / D, in1=vtmp16[:, :],
        op0=ALU.mult, op1=ALU.subtract),
        waits=((pr.vm, u3), (pr.sm, st_s)))
    u5 = pr.op("S", lambda e: e.activation(out=vtmp16[:, :], in_=vtmp16[:, :],
                                           func=AF.Ln),
               waits=((pr.vm, u4),))
    u6 = pr.op("S", lambda e: e.activation(out=rstd16[:, :], in_=vtmp16[:, :],
                                           func=AF.Exp, bias=0.0, scale=-0.5),
               waits=((pr.sm, u5),))

    elu_free = [0, 0]
    elu_s_last = [0]

    def elu_tile(bank, dst_ap_fn, buf, tm_wait):
        wl = [(pr.tm, tm_wait)]
        if elu_free[buf]:
            wl.append((pr.vm, elu_free[buf]))
        s1 = pr.op("S", lambda e, bank=bank, buf=buf: e.activation(
            out=elu_r[buf][:, :], in_=ps[bank][:, :], func=AF.Relu),
            waits=tuple(wl))
        v1 = pr.op("V", lambda e, bank=bank, buf=buf: e.tensor_sub(
            out=elu_m[buf][:, :], in0=ps[bank][:, :], in1=elu_r[buf][:, :]),
            waits=((pr.sm, s1),))
        s2 = pr.op("S", lambda e, buf=buf: e.activation(
            out=elu_m[buf][:, :], in_=elu_m[buf][:, :], func=AF.Exp),
            waits=((pr.vm, v1),))
        v2 = pr.op("V", lambda e, buf=buf, dst_ap_fn=dst_ap_fn: e.tensor_add(
            out=dst_ap_fn(), in0=elu_r[buf][:, :], in1=elu_m[buf][:, :]),
            waits=((pr.sm, s2),))
        elu_free[buf] = v2
        elu_s_last[0] = s2
        return v2

    a1_evac = []
    a1_mm = []
    q2_done = []
    vd_prev = 0
    xc_state = {"last": 0}
    pcount = 0
    xcb = [kv_z, xc_bf]

    def emit_ldsc(t):
        ld = pr.dma("Y", pr.xl[0], lambda e, t=t: e.dma_start(
            out=xtile[:, :], in_=x_d[t * P:(t + 1) * P, :]),
            waits=(((pr.sm, xc_state["last"]),) if xc_state["last"] else
                   ((pr.sm, u6), (pr.vm, u4))))
        wsc = [(pr.xl[0], ld), (pr.sm, u6)]
        if t >= 2:
            wsc.append((pr.tm, a1_mm[t - 2]))
        elif t == 1:
            wsc.append((pr.tm, w2_last_tm))
        sc = pr.op("S", lambda e, t=t: e.activation(
            out=xcb[t % 2][:, :], in_=xtile[:, :], func=AF.Identity,
            bias=nmean16[:, t:t + 1], scale=1.0),
            waits=tuple(wsc))
        xc_state["last"] = sc
        return ld, sc

    pref = {}
    for ts4 in range(4):
        for tloc in range(4):
            t = ts4 * 4 + tloc
            if t in pref:
                ld, sc = pref.pop(t)
            else:
                ld, sc = emit_ldsc(t)
            vd = pr.op("V", lambda e, t=t: e.tensor_scalar_mul(
                out=drstd[:, :], in0=ident[:, :], scalar1=rstd16[:, t:t + 1]),
                waits=((pr.sm, u6), (pr.cy, cy_all), (pr.vm, vd_prev)) +
                      (((pr.tm, a1_mm[t - 1]),) if t else ()))
            vd_prev = vd
            evl = {}
            for half in range(2):
                for i in range(half * 8, half * 8 + 8):
                    bank = 6 + (i // 4) % 2
                    wl = []
                    if i == half * 8:
                        wl = [(pr.vm, vd), (pr.sm, sc)]
                        if half == 0 and t >= 1:
                            wl.append((pr.vm, a1_evac[t - 1]))
                        if half == 0 and t == 0:
                            wl.append((pr.vm, vw2k))
                            wl.append((pr.vm, vw2q))
                        if half == 1:
                            wl.append((pr.vm, evl[0][0]))
                            wl.append((pr.sm, evl[0][1]))
                    pr.mm(lambda e, i=i, bank=bank, t=t: e.matmul(
                        ps[bank][:, (i % 4) * P:(i % 4 + 1) * P],
                        xcb[t % 2][:, i * P:(i + 1) * P], drstd[:, :],
                        start=(i % 4 == 0), stop=(i % 4 == 3)),
                        waits=tuple(wl), inc=(i % 4 == 3))
                tmv = pr.tm.n
                evv = pr.op("V", lambda e, half=half, t=t: e.tensor_copy(
                    out=xn3[:, half * 8:half * 8 + 4, t * P:(t + 1) * P],
                    in_=ps[6][:, :]),
                    waits=((pr.tm, tmv - 1),))
                evs = pr.op("S", lambda e, half=half, t=t: e.copy(
                    out=xn3[:, half * 8 + 4:half * 8 + 8, t * P:(t + 1) * P],
                    in_=ps[7][:, :]),
                    waits=((pr.tm, tmv),))
                evl[half] = (evv, evs)
            a1_mm.append(pr.tm.n)
            ev = pr.op("V", lambda e: e.memset(sq_t[:, :], 0.0),
                       waits=((pr.vm, evl[1][0]), (pr.sm, evl[1][1])))
            a1_evac.append(ev)
            wlg = [(pr.vm, ev)]
            if t == 0:
                wlg.append((pr.vm, w2ev_last))
                wlg.append((pr.cp, consts_done))
            for i in range(IC):
                t_lg = pr.mm(lambda e, i=i, t=t: e.matmul(
                    ps[5][:, t * E:(t + 1) * E],
                    xn[:, i * NTOK + t * P: i * NTOK + (t + 1) * P],
                    wg_bf[:, i * E:(i + 1) * E],
                    start=(t == 0 and i == 0), stop=(t == TT - 1 and i == IC - 1)),
                    waits=tuple(wlg) if i == 0 else (), inc=(i == IC - 1))
        for j in range(2):
            tp = (ts4 + 1) * 4 + j
            if tp < TT:
                pref[tp] = emit_ldsc(tp)
        for pack in range(8):
            bank = pcount % 4
            wl = []
            if pcount < 4:
                wl.append((pr.vm, vw2q))
                wl.append((pr.vm, a1_evac[ts4 * 4 + 3]))
            else:
                wl.append((pr.vm, q2_done[pcount - 4]))
                wl.append((pr.vm, a1_evac[ts4 * 4 + 3]))
            for i in range(IC):
                pr.mm(lambda e, i=i, pack=pack, ts4=ts4, bank=bank: e.matmul(
                    ps[bank][:, :],
                    w2q[:, i * 1024 + pack * P: i * 1024 + (pack + 1) * P],
                    xn[:, i * NTOK + ts4 * 512: i * NTOK + (ts4 + 1) * 512],
                    start=(i == 0), stop=False),
                    waits=tuple(wl) if i == 0 else ())
            tmv = pr.mm(lambda e, pack=pack, bank=bank: e.matmul(
                ps[bank][:, :], b2q_row[0:1, pack * P:(pack + 1) * P], ones[0:1, 0:512],
                start=False, stop=True), inc=True)
            dv = elu_tile(bank, (lambda pack=pack, ts4=ts4:
                                 q2[:, pack * NTOK + ts4 * 512: pack * NTOK + (ts4 + 1) * 512]),
                          pcount % 2, tmv)
            q2_done.append(dv)
            pcount += 1
    a1_s_last = pr.sm.n
    t_lg_last = t_lg
    if STOP == "a1":
        pr.quiesce()
        return pr
    v_pre = pr.op("V", lambda e: e.memset(g9bf[:, 8:9], 1.0))
    g_done = []
    gt_done = []
    tg1_list = []
    lg3 = lg_all.ap().rearrange("p (t e) -> p t e", e=E)
    mt3 = mtB.ap().rearrange("p (t e) -> p t e", e=E)
    ca3 = caB.ap().rearrange("p (t e) -> p t e", e=E)
    cb3 = cbB.ap().rearrange("p (t e) -> p t e", e=E)
    cc3g = ccB.ap().rearrange("p (t e) -> p t e", e=E)
    et3 = etB.ap().rearrange("p (t e) -> p t e", e=E)
    ga3 = g_all.ap().rearrange("p (t e) -> p t e", e=E)
    m16b = m16.ap().rearrange("p (t e) -> p t e", e=1).to_broadcast([P, TT, E])
    g16b = gate16.ap().rearrange("p (t e) -> p t e", e=1).to_broadcast([P, TT, E])
    gc0 = pr.op("V", lambda e: e.tensor_copy(out=lg_all[:, :], in_=ps[5][:, 0:TT * E]),
               waits=((pr.tm, t_lg_last), (pr.sm, a1_s_last)))
    gate_ps_read = gc0
    gc1 = pr.op("V", lambda e: e.reduce_max(out=m16[:, :], in_=lg3, axis=AX.X),
               waits=((pr.vm, gc0),))
    gc2 = pr.op("V", lambda e: e.tensor_tensor(out=mt3, in0=lg3, in1=m16b,
                                              op=ALU.is_equal),
               waits=((pr.vm, gc1),))
    gc3 = pr.op("V", lambda e: e.tensor_copy(out=caB[:, :], in_=mtB[:, :]),
               waits=((pr.vm, gc2),))
    gc4 = pr.op("V", lambda e: e.tensor_add(out=ca3[:, :, 1:E], in0=mt3[:, :, 1:E],
                                           in1=mt3[:, :, 0:E - 1]),
               waits=((pr.vm, gc3),))
    gc5 = pr.op("V", lambda e: e.tensor_copy(out=cbB[:, :], in_=caB[:, :]),
               waits=((pr.vm, gc4),))
    gc6 = pr.op("V", lambda e: e.tensor_add(out=cb3[:, :, 2:E], in0=ca3[:, :, 2:E],
                                           in1=ca3[:, :, 0:E - 2]),
               waits=((pr.vm, gc5),))
    gc7 = pr.op("V", lambda e: e.tensor_copy(out=ccB[:, :], in_=cbB[:, :]),
               waits=((pr.vm, gc6),))
    gc8 = pr.op("V", lambda e: e.tensor_add(out=cc3g[:, :, 4:E], in0=cb3[:, :, 4:E],
                                           in1=cb3[:, :, 0:E - 4]),
               waits=((pr.vm, gc7),))
    gc9 = pr.op("V", lambda e: e.tensor_scalar(out=ccB[:, :], in0=ccB[:, :],
                                              scalar1=1.0, scalar2=None,
                                              op0=ALU.is_equal),
               waits=((pr.vm, gc8),))
    gc10 = pr.op("V", lambda e: e.tensor_mul(out=mtB[:, :], in0=mtB[:, :],
                                            in1=ccB[:, :]),
                waits=((pr.vm, gc9),))
    gc11 = pr.op("V", lambda e: e.tensor_tensor(out=et3, in0=lg3, in1=m16b,
                                               op=ALU.subtract),
                waits=((pr.vm, gc2),))
    sB = pr.op("S", lambda e: e.activation(out=etB[:, :], in_=etB[:, :], func=AF.Exp),
               waits=((pr.vm, gc11),))
    gc12 = pr.op("V", lambda e: e.reduce_sum(out=ssum16[:, :], in_=et3, axis=AX.X),
                waits=((pr.sm, sB),))
    gc13 = pr.op("V", lambda e: e.reciprocal(out=gate16[:, :], in_=ssum16[:, :]),
                waits=((pr.vm, gc12),))
    gB = pr.op("V", lambda e: e.tensor_tensor(out=ga3, in0=mt3, in1=g16b,
                                              op=ALU.mult),
               waits=((pr.vm, gc13), (pr.vm, gc10)))
    def emit_gt(t, first_wait=()):
        bank = 6 + t % 2
        c15 = pr.op("V", lambda e, t=t: e.tensor_copy(out=g9bf[:, 0:E],
                                                      in_=g_all[:, t * E:(t + 1) * E]),
                    waits=((pr.vm, gB), (pr.vm, v_pre)) +
                          (((pr.tm, tg1_list[t - 1]),) if t else ()))
        tg1 = pr.mm(lambda e, bank=bank: e.matmul(
            ps[bank][0:9, 0:P], g9bf[:, 0:9], ident[:, :], start=True, stop=True),
            waits=((pr.vm, c15),) + tuple(first_wait) +
                  (((pr.vm, gt_done[t - 2]),) if t >= 2 else ()),
            inc=True)
        tg1_list.append(tg1)
        gv = pr.op("V", lambda e, t=t, bank=bank: e.tensor_copy(
            out=gT9[:, t * P:(t + 1) * P], in_=ps[bank][0:9, 0:P]),
            waits=((pr.tm, tg1),))
        g_done.append(gB)
        gt_done.append(gv)


    if STOP == "a1b":
        pr.quiesce()
        return pr



    k2_done = []
    kcount = 0
    k2_tm_last = [0]
    ks_first = [True]

    def emit_ksum_tile(tk):
        for packk in range(8):
            wlks = []
            if packk == 0:
                wlks.append((pr.vm, k2_done[2 * tk + 1]))
                if ks_first[0]:
                    wlks.append((pr.cy, cy_all))
                    ks_first[0] = False
            pr.mm(lambda e, tk=tk, packk=packk: e.matmul(
                ps[6][:, packk:packk + 1],
                k2[:, tk * 1024 + packk * P: tk * 1024 + (packk + 1) * P],
                onec[:, :],
                start=(tk == 0 and packk == 0), stop=(tk == TT - 1 and packk == 7)),
                waits=tuple(wlks), inc=(tk == TT - 1 and packk == 7))

    for t in range(TT):
        for half in range(2):
            bank = kcount % 4
            wl = []
            if kcount < 4:
                wl.append((pr.vm, vw2k))
                wl.append((pr.vm, q2_done[28 + kcount]))
            else:
                wl.append((pr.vm, k2_done[kcount - 4]))
            for i in range(IC):
                pr.mm(lambda e, i=i, t=t, half=half, bank=bank: e.matmul(
                    ps[bank][:, :],
                    xn[:, i * NTOK + t * P: i * NTOK + (t + 1) * P],
                    w2k[:, i * 1024 + half * 512: i * 1024 + (half + 1) * 512],
                    start=(i == 0), stop=False),
                    waits=tuple(wl) if i == 0 else ())
            tmv = pr.mm(lambda e, half=half, bank=bank: e.matmul(
                ps[bank][:, :], ones[0:1, 0:P], b2k_row[0:1, half * 512:(half + 1) * 512],
                start=False, stop=True), inc=True)
            k2_tm_last[0] = tmv
            dv = elu_tile(bank, (lambda t=t, half=half:
                                 k2[:, t * 1024 + half * 512: t * 1024 + (half + 1) * 512]),
                          kcount % 2, tmv)
            k2_done.append(dv)
            kcount += 1
        if t >= 1:
            emit_ksum_tile(t - 1)
    emit_ksum_tile(TT - 1)
    t_ksum = pr.tm.n
    v_ks = pr.op("V", lambda e: e.tensor_copy(out=kv_f32[:, 1024:1032], in_=ps[6][:, 0:8]),
                 waits=((pr.tm, t_ksum),))

    if STOP == "a2k":
        pr.quiesce()
        return pr
    attnf3 = attn_f.ap().rearrange("p (h t) -> p h t", t=1024)
    x4 = x_d.ap().rearrange("(t p) o -> p t o", p=P)
    o4 = out_d.ap().rearrange("(t p) o -> p t o", p=P)
    wo4 = woT_d.ap().rearrange("(i p) o -> p i o", p=P)
    wt4 = wT_d.ap().rearrange("e (i p) o -> e p i o", p=P)
    acc_free = [0, 0]
    wffn_free = [0, 0]
    ffn_idx = [0]
    first_wload = [True]
    pending_init = {}

    def emit_init(tgx, ohx):
        abx = (tgx * NOH + ohx) % 2
        wacc = []
        if acc_free[abx]:
            wacc.append((pr.asn[abx], acc_free[abx]))
        else:
            wacc.append((pr.tm, grp_last_mm[2 + abx]))
        return pr.dma("Y", pr.ai[abx], lambda e, tgx=tgx, ohx=ohx, abx=abx: e.dma_start(
            out=accs[abx].ap().rearrange("p (t o) -> p t o", o=512),
            in_=x4[:, tgx * 8:(tgx + 1) * 8, ohx * 512:(ohx + 1) * 512]),
            waits=tuple(wacc))

    def load_expert(ee, oh, extra_waits=(), extra_waits_b1=()):
        b0 = ffn_idx[0] % 2
        ffn_idx[0] += 1
        b1 = ffn_idx[0] % 2
        ffn_idx[0] += 1
        wl0 = list(extra_waits)
        if wffn_free[b0]:
            wl0.append((pr.tm, wffn_free[b0]))
        if first_wload[0]:
            wl0.append((pr.sm, a1_s_last))
            wl0.append((pr.vm, gB))
            first_wload[0] = False
        ld0 = pr.dma("P", pr.wf[b0], lambda e, ee=ee, oh=oh, b0=b0: e.dma_start(
            out=wffn[b0].ap().rearrange("p (i o) -> p i o", o=512),
            in_=wt4[ee, :, 0:8, oh * 512:(oh + 1) * 512]),
            waits=tuple(wl0))
        wl1 = list(extra_waits) + list(extra_waits_b1)
        if wffn_free[b1]:
            wl1.append((pr.tm, wffn_free[b1]))
        ld1 = pr.dma("P", pr.wf[b1], lambda e, ee=ee, oh=oh, b1=b1: e.dma_start(
            out=wffn[b1].ap().rearrange("p (i o) -> p i o", o=512),
            in_=wt4[ee, :, 8:16, oh * 512:(oh + 1) * 512]),
            waits=tuple(wl1))
        return (b0, ld0, b1, ld1)

    def emit_expert(tg, oh, ee, acc, prev_ev, ab, d_init, preloaded=None):
        if preloaded is None:
            b0, ld0, b1, ld1 = load_expert(ee, oh)
        else:
            b0, ld0, b1, ld1 = preloaded
        tv_tl = [0] * 8
        for tl in range(8):
            for i in range(8):
                wl5 = []
                if i == 0:
                    wl5.append((pr.vm, prev_ev[tl]))
                    wl5.append((pr.ai[ab], d_init))
                    if tl == 0:
                        wl5.append((pr.wf[b0], ld0))
                pr.mm(lambda e, i=i, tl=tl, b0=b0, tg=tg: e.matmul(
                    ps[tl][:, :],
                    xn[:, i * NTOK + (tg * 8 + tl) * P: i * NTOK + (tg * 8 + tl + 1) * P],
                    wffn[b0][:, (i % 8) * 512:(i % 8 + 1) * 512],
                    start=(i == 0), stop=False),
                    waits=tuple(wl5), inc=(tl == 7 and i == 7))
        wffn_free[b0] = pr.tm.n
        new_ev = []
        for tl in range(8):
            for i in range(8, IC):
                wl5 = []
                if i == 8 and tl == 0:
                    wl5.append((pr.wf[b1], ld1))
                v = pr.mm(lambda e, i=i, tl=tl, b1=b1, tg=tg: e.matmul(
                    ps[tl][:, :],
                    xn[:, i * NTOK + (tg * 8 + tl) * P: i * NTOK + (tg * 8 + tl + 1) * P],
                    wffn[b1][:, (i % 8) * 512:(i % 8 + 1) * 512],
                    start=False, stop=(i == IC - 1)),
                    waits=tuple(wl5), inc=(i == IC - 1))
                if i == IC - 1:
                    tv_tl[tl] = v
            ev = pr.op("V", lambda e, tl=tl, ee=ee, tg=tg, acc=acc: e.scalar_tensor_tensor(
                out=acc[:, tl * 512:(tl + 1) * 512],
                in0=ps[tl][:, :],
                scalar=g_all[:, (tg * 8 + tl) * E + ee: (tg * 8 + tl) * E + ee + 1],
                in1=acc[:, tl * 512:(tl + 1) * 512],
                op0=ALU.mult, op1=ALU.add),
                waits=((pr.tm, tv_tl[tl]), (pr.vm, g_done[tg * 8 + tl])))
            new_ev.append(ev)
        wffn_free[b1] = pr.tm.n
        return new_ev

    wv4 = wvT_d.ap().rearrange("(i p) o -> p i o", p=P)
    wvg3s = [w.ap().rearrange("p (i o) -> p i o", o=512) for w in wvg]
    kv_mm_last = None
    kv_grp_last = [0, 0, 0, 0]
    kv_ev_last = [0]
    hoist_lds = []
    vv_hist = []
    first_vv = True
    grp_last_mm = [0, 0, 0, 0]
    for grp in range(4):
        wb = grp % 2
        wvgrp = wvg[wb]
        ld = pr.dma("P", pr.wv[wb], lambda e, grp=grp, wb=wb: e.dma_start(
            out=wvg3s[wb][:, :, :], in_=wv4[:, :, grp * 512:(grp + 1) * 512]),
            waits=(((pr.tm, k2_tm_last[0]),) if grp < 2
                   else ((pr.tm, grp_last_mm[grp - 2]),)))
        if grp == 1:
            hb0 = ffn_idx[0] % 2
            ffn_idx[0] += 1
            hb1 = ffn_idx[0] % 2
            ffn_idx[0] += 1
            hld0 = pr.dma("P", pr.wf[hb0], lambda e, hb0=hb0: e.dma_start(
                out=wffn[hb0].ap().rearrange("p (i o) -> p i o", o=512),
                in_=wt4[0, :, 0:8, 0:512]),
                waits=((pr.sm, a1_s_last), (pr.vm, gB)))
            first_wload[0] = False
            hoist_lds.append((hb0, hld0, hb1))
        for t in range(TT):
            bank = t % 2
            wl = []
            if grp == 0 and t < 2:
                wl.append((pr.vm, k2_done[28 + t]))
            if t == 0:
                wl.append((pr.wv[wb], ld))
            if len(vv_hist) >= 2:
                wl.append((pr.vm, vv_hist[-2]))
            for i in range(IC):
                pr.mm(lambda e, i=i, t=t, bank=bank, wvgrp=wvgrp: e.matmul(
                    ps[bank][:, :],
                    xn[:, i * NTOK + t * P: i * NTOK + (t + 1) * P],
                    wvgrp[:, i * 512:(i + 1) * 512],
                    start=(i == 0), stop=False),
                    waits=tuple(wl) if i == 0 else ())
            tv = pr.mm(lambda e, grp=grp, bank=bank: e.matmul(
                ps[bank][:, :], ones[0:1, 0:P], bv_row[0:1, grp * 512:(grp + 1) * 512],
                start=False, stop=True), inc=True)
            grp_last_mm[grp] = tv
            wlv = [(pr.tm, tv)]
            if first_vv:
                wlv.append((pr.sm, elu_s_last[0]))
                first_vv = False
            vv = pr.op("V", lambda e, bank=bank: e.tensor_copy(
                out=asb[bank][:, 0:512], in_=ps[bank][:, :]), waits=tuple(wlv))
            vv_hist.append(vv)

            def emit_kv(tk):
                nonlocal kv_mm_last
                bk = tk % 2
                for local in range(2):
                    pk = grp * 2 + local
                    wlk = []
                    if local == 0:
                        wlk.append((pr.vm, vv_hist[-1 if tk == TT - 1 else -2]))
                    if grp == 2 and tk == 0 and local == 0:
                        wlk.append((pr.vm, v_ks))
                    kv_mm_last = pr.mm(lambda e, tk=tk, pk=pk, local=local, bk=bk,
                                       grp=grp: e.matmul(
                        ps[4 + grp][:, local * 256:(local + 1) * 256],
                        k2[:, tk * 1024 + pk * P: tk * 1024 + (pk + 1) * P],
                        asb[bk][:, local * 256:(local + 1) * 256],
                        start=(tk == 0 and local == 0),
                        stop=(tk == TT - 1 and local == 1)),
                        waits=tuple(wlk), inc=(local == 1))
                    if tk == TT - 1 and local == 1:
                        kv_grp_last[grp] = kv_mm_last

            if t > 0:
                emit_kv(t - 1)
            if t == TT - 1:
                emit_kv(t)
        for rh in range(2):
            src_ap = ps[4 + grp].ap().rearrange("p (l d) -> p l d", d=P)
            kv_ev_last[0] = pr.op("V", lambda e, g=grp, rh=rh, src_ap=src_ap: e.tensor_copy(
                out=kv_f32[rh * 64:(rh + 1) * 64, g * 256:(g + 1) * 256],
                in_=src_ap[rh * 64:(rh + 1) * 64, rh::2, :]),
                waits=((pr.tm, kv_grp_last[grp]),) if rh == 0 else ())
    v_kv1 = kv_ev_last[0]

    if STOP == "a3":
        pr.quiesce()
        return pr
    d_kv = pr.dma("P", pr.kva, lambda e: e.dma_start(out=cc_in[:, :], in_=kv_f32[:, :]),
                  waits=((pr.vm, v_kv1), (pr.vm, v_ks)))
    hb0, hld0, hb1 = hoist_lds[0]
    hld1 = pr.dma("P", pr.wf[hb1], lambda e, hb1=hb1: e.dma_start(
        out=wffn[hb1].ap().rearrange("p (i o) -> p i o", o=512),
        in_=wt4[0, :, 8:16, 0:512]),
        waits=((pr.kva, d_kv),))
    pr.emit("P", lambda e: e.collective_compute(
        "AllReduce", ALU.add,
        replica_groups=[[0, 1], [2, 3], [4, 5], [6, 7]],
        ins=[cc_in.ap().opt()], outs=[cc_out.ap().opt()]),
        waits=((pr.kva, d_kv),), incs=((pr.cm, 1),))
    pr.cm.n = 1
    z1 = pr.op("V", lambda e: e.memset(kv_z[:, :], 0.0))
    z2 = pr.op("V", lambda e: e.memset(ksum_z[:, :], 0.0), waits=((pr.vm, z1),))
    for t in range(TT):
        emit_gt(t, first_wait=(((pr.vm, v_kv1),) if t == 0 else ()))
    d_init00 = emit_init(0, 0)
    pending_init[(0, 0)] = d_init00
    hoist_pe = [z2] * 6 + [gt_done[14], gt_done[15]]
    hoist_pe = emit_expert(0, 0, 0, accs[0], hoist_pe, 0, d_init00,
                           preloaded=(hb0, hld0, hb1, hld1))
    hoist_pe = emit_expert(0, 0, 1, accs[0], hoist_pe, 0, d_init00)

    cc3 = cc_out.ap()[:, 0:1024].rearrange("p (hp d) -> p hp d", d=P)
    kvz3 = kv_z.ap().rearrange("p (hp d) -> p hp d", d=2 * P)
    pr.dma("P", pr.kvb, lambda e: e.dma_start(
        out=kvz3[0:64, :, 0:P], in_=cc3[0:64, 0:8, :]),
        waits=((pr.cm, 1), (pr.vm, z2)))
    pr.dma("P", pr.kvb, lambda e: e.dma_start(
        out=kvz3[64:128, :, P:2 * P], in_=cc3[64:128, 0:8, :]))
    ksz3 = ksum_z.ap().rearrange("p (hp c) -> p hp c", c=2)
    ko3 = cc_out.ap()[:, 1024:1032].rearrange("p (x c) -> p x c", c=1)
    pr.dma("P", pr.kvb, lambda e: e.dma_start(
        out=ksz3[0:64, :, 0:1], in_=ko3[0:64, :, :],
        allow_slow_non_contiguous=True))
    kvb_all = pr.dma("P", pr.kvb, lambda e: e.dma_start(
        out=ksz3[64:128, :, 1:2], in_=ko3[64:128, :, :],
        allow_slow_non_contiguous=True))

    if STOP == "cc":
        pr.quiesce()
        return pr
    cross_tg_guard = 0
    c2_seed = None
    for tg in range(NTG):
        if tg == 0:
            cross_tg_guard = hoist_pe[7]
            c2_seed = hoist_pe
        sgq_hist = []
        r1_par = [0, 0]
        r3_tl = [0] * 8
        scl_v = [0] * 8
        scl_s = [0] * 8
        ev_hist = []

        def c1_norm(tl):
            t = tg * 8 + tl
            nb = 0 if tl % 2 == 0 else 5
            wl = []
            if tl == 0 and tg == 0:
                wl.append((pr.kvb, kvb_all))
            if tl <= 1 and cross_tg_guard:
                wl.append((pr.vm, cross_tg_guard))
            if r1_par[tl % 2]:
                wl.append((pr.vm, r1_par[tl % 2]))
            for pack in range(8):
                pr.mm(lambda e, pack=pack, t=t, nb=nb: e.matmul(
                    ps[nb][:, pack * 2:(pack + 1) * 2],
                    q2[:, pack * NTOK + t * P: pack * NTOK + (t + 1) * P],
                    ksum_z[:, pack * 2:(pack + 1) * 2],
                    start=(pack == 0), stop=(pack == 7)),
                    waits=tuple(wl) if pack == 0 else (), inc=(pack == 7))
            t_nm = pr.tm.n
            rn = rnorm2[tl % 2]
            wr = [(pr.tm, t_nm)]
            if tl >= 2:
                wr.append((pr.sm, scl_s[tl - 2]))
                wr.append((pr.vm, scl_v[tl - 2]))
            r1 = pr.op("V", lambda e, nb=nb, rn=rn: e.tensor_copy(
                out=rn[:, :], in_=ps[nb][:, 0:H]), waits=tuple(wr))
            r2 = pr.op("V", lambda e, rn=rn: e.tensor_scalar(
                out=rn[:, :], in0=rn[:, :], scalar1=EPS_AT, scalar2=None,
                op0=ALU.add), waits=((pr.vm, r1),))
            r3 = pr.op("V", lambda e, rn=rn: e.reciprocal(out=rn[:, :], in_=rn[:, :]),
                       waits=((pr.vm, r2),))
            r1_par[tl % 2] = r1
            r3_tl[tl] = r3

        def c1_attn(tl):
            t = tg * 8 + tl
            rn = rnorm2[tl % 2]
            for gq in range(4):
                abank = 1 + gq % 2
                wl2 = [(pr.vm, r3_tl[tl])]
                if len(sgq_hist) >= 2:
                    pv, psv = sgq_hist[-2]
                    wl2.append((pr.vm, pv))
                    wl2.append((pr.sm, psv))
                for lp in range(2):
                    pack = gq * 2 + lp
                    pr.mm(lambda e, lp=lp, pack=pack, t=t, abank=abank: e.matmul(
                        ps[abank][:, lp * 256:(lp + 1) * 256],
                        q2[:, pack * NTOK + t * P: pack * NTOK + (t + 1) * P],
                        kv_z[:, pack * 256:(pack + 1) * 256],
                        start=(lp == 0), stop=(lp == 1)),
                        waits=tuple(wl2) if lp == 0 else (), inc=(lp == 1))
                t_at = pr.tm.n
                if gq % 2 == 0:
                    sc = 0
                    for j in range(4):
                        h = gq * 4 + j
                        sc = pr.op("V", lambda e, h=h, j=j, abank=abank, tl=tl, rn=rn: e.tensor_scalar_mul(
                            out=asb[tl % 2][:, h * P:(h + 1) * P],
                            in0=ps[abank][:, j * P:(j + 1) * P],
                            scalar1=rn[:, h:h + 1]),
                            waits=((pr.tm, t_at),) if j == 0 else ())
                    sgq_hist.append((sc, 0))
                    scl_v[tl] = sc
                else:
                    sc = 0
                    for j in range(4):
                        h = gq * 4 + j
                        sc = pr.op("S", lambda e, h=h, j=j, abank=abank, tl=tl, rn=rn: e.activation(
                            out=asb[tl % 2][:, h * P:(h + 1) * P],
                            in_=ps[abank][:, j * P:(j + 1) * P],
                            func=AF.Copy, scale=rn[:, h:h + 1]),
                            waits=((pr.tm, t_at),) if j == 0 else ())
                    sgq_hist.append((0, sc))
                    scl_s[tl] = sc

        def c1_transp(tl):
            for gq in range(4):
                tbank = 3 + gq % 2
                wl3 = [(pr.vm, scl_v[tl]), (pr.sm, scl_s[tl])]
                if len(ev_hist) >= 2:
                    wl3.append((pr.vm, ev_hist[-2]))
                for j in range(4):
                    h = gq * 4 + j
                    pr.mm(lambda e, h=h, j=j, tbank=tbank, tl=tl: e.matmul(
                        ps[tbank][:, j * P:(j + 1) * P],
                        asb[tl % 2][:, h * P:(h + 1) * P], ident[:, :],
                        start=(j == 0), stop=(j == 3)),
                        waits=tuple(wl3) if j == 0 else (), inc=(j == 3))
                t_tr = pr.tm.n
                ev = pr.op("V", lambda e, gq=gq, tbank=tbank, tl=tl: e.tensor_copy(
                    out=attnf3[:, gq * 4:(gq + 1) * 4, tl * P:(tl + 1) * P],
                    in_=ps[tbank].ap().rearrange("p (j d) -> p j d", d=P)),
                    waits=((pr.tm, t_tr),))
                ev_hist.append(ev)

        c1_norm(0)
        c1_norm(1)
        c1_attn(0)
        for tl in range(1, 8):
            if tl + 1 < 8:
                c1_norm(tl + 1)
            c1_attn(tl)
            c1_transp(tl - 1)
        c1_transp(7)
        c1_last = pr.vm.n
        if STOP in ("c1", "c1n", "c1a"):
            pr.quiesce()
            return pr

        last_ps_reader = [c1_last] * 8

        for oh in range(NOH):
            ab = (tg * NOH + oh) % 2
            acc = accs[ab]
            key = (tg, oh)
            if key in pending_init:
                d_init = pending_init.pop(key)
            else:
                d_init = emit_init(tg, oh)
            ohalf = []
            for halfw in range(2):
                b = ffn_idx[0] % 2
                ffn_idx[0] += 1
                wlw = []
                if wffn_free[b]:
                    wlw.append((pr.tm, wffn_free[b]))
                ld = pr.dma("P", pr.wf[b], lambda e, halfw=halfw, oh=oh, b=b: e.dma_start(
                    out=wffn[b].ap().rearrange("p (i o) -> p i o", o=512),
                    in_=wo4[:, halfw * 8:(halfw + 1) * 8, oh * 512:(oh + 1) * 512]),
                    waits=tuple(wlw))
                ohalf.append((b, ld))
            for idx, (b, ld) in enumerate(ohalf):
                for tl in range(8):
                    for ii in range(8):
                        i = idx * 8 + ii
                        wl4 = []
                        if i == 0:
                            wl4 += [(pr.vm, last_ps_reader[tl]), (pr.ai[ab], d_init)]
                        if ii == 0 and tl == 0:
                            wl4.append((pr.wf[b], ld))
                        pr.mm(lambda e, i=i, ii=ii, tl=tl, b=b, tg=tg: e.matmul(
                            ps[tl][:, :],
                            attn_f[:, i * 1024 + tl * P: i * 1024 + (tl + 1) * P],
                            wffn[b][:, ii * 512:(ii + 1) * 512],
                            start=(i == 0), stop=False),
                            waits=tuple(wl4),
                            inc=(tl == 7 and ii == 7))
                wffn_free[b] = pr.tm.n
            prev_ev = []
            for tl in range(8):
                tv = pr.mm(lambda e, tl=tl, tg=tg, oh=oh: e.matmul(
                    ps[tl][:, :], gT9[:, (tg * 8 + tl) * P:(tg * 8 + tl + 1) * P],
                    b9[0:9, oh * 512:(oh + 1) * 512],
                    start=False, stop=True),
                    waits=((pr.vm, gt_done[tg * 8 + tl]), (pr.cp, consts_done)),
                    inc=True)
                ev = pr.op("V", lambda e, tl=tl, acc=acc: e.tensor_add(
                    out=acc[:, tl * 512:(tl + 1) * 512],
                    in0=ps[tl][:, :], in1=acc[:, tl * 512:(tl + 1) * 512]),
                    waits=((pr.tm, tv),))
                prev_ev.append(ev)
            e_start = 2 if (tg == 0 and oh == 0) else 0
            if e_start == 2:
                prev_ev = [max(a, b) for a, b in zip(prev_ev, c2_seed)]
            for ee in range(e_start, E):
                prev_ev = emit_expert(tg, oh, ee, acc, prev_ev, ab, d_init)
            last_ps_reader = [v for v in prev_ev]
            seq = [(tgx, ohx) for tgx in range(NTG) for ohx in range(NOH)]
            pos = seq.index((tg, oh))
            if pos + 1 < len(seq):
                nxt = seq[pos + 1]
                pending_init[nxt] = emit_init(*nxt)
            st = pr.dma("Y", pr.asn[ab], lambda e, tg=tg, oh=oh, acc=acc: e.dma_start(
                out=o4[:, tg * 8:(tg + 1) * 8, oh * 512:(oh + 1) * 512],
                in_=acc.ap().rearrange("p (t o) -> p t o", o=512)),
                waits=tuple((pr.vm, v) for v in prev_ev))
            acc_free[ab] = st
            cross_tg_guard = last_ps_reader[7]

    f0, f1 = pr.asn[0].n, pr.asn[1].n
    pr.emit("Y", lambda e, f0=f0: e.wait_ge(pr.asn[0].h, f0))
    pr.emit("Y", lambda e, f1=f1: e.wait_ge(pr.asn[1].h, f1))
    return pr


def _build_nc():
    nc = bass.Bass("TRN2", target_bir_lowering=False)
    pr = build(nc)
    body = pr.realize()
    with nc.Block() as block:
        block.tensor(body("T"))
        block.vector(body("V"))
        block.scalar(body("S"))
        block.gpsimd(body("P"))
        block.sync(body("Y"))
    return nc


def kernel(L0_attn_hidden_states, L0_attn_wq, L0_attn_bq, L0_attn_wk, L0_attn_bk,
           L0_attn_wv, L0_attn_bv, L0_attn_wo, L0_attn_bo, L0_attn_wmap_q,
           L0_attn_wmap_k, L0_ffn_wg, L0_ffn_W, L0_ffn_b):
    x = np.ascontiguousarray(np.asarray(L0_attn_hidden_states, dtype=np.float32))
    wq = np.ascontiguousarray(np.asarray(L0_attn_wq, dtype=np.float32))
    wk = np.ascontiguousarray(np.asarray(L0_attn_wk, dtype=np.float32))
    wvT = np.ascontiguousarray(np.asarray(L0_attn_wv, dtype=np.float32).T)
    woT = np.ascontiguousarray(np.asarray(L0_attn_wo, dtype=np.float32).T)
    wT = np.ascontiguousarray(np.asarray(L0_ffn_W, dtype=np.float32).transpose(0, 2, 1))
    wmapqT = np.ascontiguousarray(np.asarray(L0_attn_wmap_q, dtype=np.float32).T)
    wmapkT = np.ascontiguousarray(np.asarray(L0_attn_wmap_k, dtype=np.float32).T)
    wg = np.ascontiguousarray(np.asarray(L0_ffn_wg, dtype=np.float32))
    be = np.ascontiguousarray(np.asarray(L0_ffn_b, dtype=np.float32))
    bq = np.ascontiguousarray(np.asarray(L0_attn_bq, dtype=np.float32).reshape(1, D))
    bk = np.ascontiguousarray(np.asarray(L0_attn_bk, dtype=np.float32).reshape(1, D))
    bv = np.ascontiguousarray(np.asarray(L0_attn_bv, dtype=np.float32).reshape(1, D))
    bo = np.ascontiguousarray(np.asarray(L0_attn_bo, dtype=np.float32).reshape(1, D))

    nc = _build_nc()
    in_maps = []
    for c in range(8):
        b, half = c // 2, c % 2
        xs = np.ascontiguousarray(x[b, half * NTOK:(half + 1) * NTOK, :])
        in_maps.append({
            "x": xs, "wq": wq, "wk": wk, "wvT": wvT, "woT": woT, "wT": wT,
            "wmapqT": wmapqT, "wmapkT": wmapkT, "wg": wg,
            "bq": bq, "bk": bk, "bv": bv, "bo": bo, "be": be,
        })
    res = run_bass_kernel_spmd(nc, in_maps, core_ids=list(range(8)))
    out = np.empty((4, 4096, D), dtype=np.float32)
    for c in range(8):
        b, half = c // 2, c % 2
        out[b, half * NTOK:(half + 1) * NTOK, :] = res.results[c]["out"]
    return out
